# version 16
# baseline (speedup 1.0000x reference)
import math
import numpy as np
import ml_dtypes
from contextlib import ExitStack

import concourse.bass as bass
import concourse.mybir as mybir
from concourse.bass_utils import run_bass_kernel_spmd

F32 = mybir.dt.float32
BF16 = mybir.dt.bfloat16
I32 = mybir.dt.int32
AX = mybir.AxisListType
ALU = mybir.AluOpType
ACT = mybir.ActivationFunctionType

S = 2048
E = 2048
H = 16
DKV = 128
DC = 64
DR = 64
DH = 128
INTER = 5632
NI = INTER // 128
NE = E // 128
HALF = S // 2

INV_SQRT_DH = float(np.float32(1.0 / math.sqrt(DH)))
NEG_BIG = -1.0e38
EPS = float(np.float32(1e-5))
TWO_PI = 2.0 * math.pi
C1 = float(np.float32(6.28125))
C2 = float(np.float32(TWO_PI - np.float64(np.float32(6.28125))))
C3 = float(np.float32(TWO_PI - np.float64(np.float32(6.28125)) - np.float64(np.float32(C2))))
INV2PI = float(np.float32(1.0 / TWO_PI))
PI_F = float(np.float32(math.pi))
PI_DN = float(np.nextafter(np.float32(math.pi), np.float32(0)))
A1 = float(np.float32(math.pi / 2))
A2 = float(np.float32(math.pi / 2 - np.float64(np.float32(math.pi / 2))))

FREQ_BITS = [
    0x3f800000, 0x3f3ff911, 0x3f0ff59a, 0x3ed7e89a, 0x3ea1e89b, 0x3e72d422, 0x3e361887, 0x3e088d77,
    0x3dcccccc, 0x3d99940d, 0x3d6655c1, 0x3d2cba16, 0x3d0186e2, 0x3cc2434e, 0x3c91ad3a, 0x3c5a7bf1,
    0x3c23d70a, 0x3bf5b9ad, 0x3bb8449a, 0x3b8a2e75, 0x3b4f3e33, 0x3b1b690e, 0x3ae91528, 0x3aaec98d,
    0x3a83126e, 0x3a44948a, 0x3a136a14, 0x39dd1722, 0x39a5cb60, 0x3978a80d, 0x393a7753, 0x390bd46d,
]


_STOP_AFTER_FFN = False
_DEBUG_DUMPS = False
_DEBUG_TILE0 = False
_DEBUG_ATT = False
_DEBUG_WO = False
_DEBUG_CANARY = False
_DEBUG_NOATTN = False
_DEBUG_PSUMONLY = False
_DEBUG_SPLITADD = False
_DEBUG_STTADD = False
_PARANOID = False
_NOPS = 0


class Sem:
    def __init__(self, handle, name):
        self.h = handle
        self.name = name
        self.n = 0


class Eng:
    def __init__(self, name, sem):
        self.name = name
        self.sem = sem
        self.ops = []
        self.seen = {}
        self.trace = []
        self.drained = 0

    def wait(self, sem, val):
        if val <= 0 or sem is self.sem:
            return
        if self.seen.get(sem.name, 0) >= val:
            return
        self.seen[sem.name] = val
        self.trace.append(("w", sem.name, val))
        h = sem.h
        self.ops.append(lambda e: e.wait_ge(h, val))

    def drain(self):
        if self.sem.n <= 0 or self.drained >= self.sem.n:
            return
        self.drained = self.sem.n
        h, val = self.sem.h, self.sem.n
        self.trace.append(("w", self.sem.name, val))
        self.ops.append(lambda e: e.wait_ge(h, val))

    def op(self, fn):
        self.ops.append(fn)

    def op_inc(self, fn, k=1):
        self.sem.n += k
        h, val = self.sem.h, self.sem.n
        self.trace.append(("i", self.sem.name, k))
        self.ops.append(lambda e: fn(e).then_inc(h, k))
        self.seen[self.sem.name] = self.sem.n
        return (self.sem, val)


class Buf:
    __slots__ = ("name", "w", "rs")

    def __init__(self, name=""):
        self.name = name
        self.w = None
        self.rs = {}


def dep_read(eng, *bufs):
    for b in bufs:
        if b.w is not None:
            if _PARANOID and b.w[0] is eng.sem and eng.drained < b.w[1]:
                eng.drain()
            eng.wait(*b.w)


def dep_write(eng, *bufs):
    for b in bufs:
        if b.w is not None:
            if _PARANOID and b.w[0] is eng.sem and eng.drained < b.w[1]:
                eng.drain()
            eng.wait(*b.w)
        for sem, val in b.rs.values():
            if _PARANOID and sem is eng.sem and eng.drained < val:
                eng.drain()
            eng.wait(sem, val)


def note_write(mark, *bufs):
    for b in bufs:
        b.w = mark
        b.rs = {}


def note_read(mark, *bufs):
    sem, val = mark
    for b in bufs:
        old = b.rs.get(sem.name)
        if old is None or old[1] < val:
            b.rs[sem.name] = (sem, val)


def build_nc():
    nc = bass.Bass("TRN2", target_bir_lowering=False, detect_race_conditions=False)

    dt = lambda name, shape, dtype=F32, kind="ExternalInput": \
        nc.dram_tensor(name, shape, dtype, kind=kind).ap()

    x_h = dt("x", [S, E])
    wdkv_h = dt("wdkv", [E, DKV])
    wdq_h = dt("wdq", [E, DKV])
    wkr_h = dt("wkr", [E, DR])
    wqc_h = dt("wqc", [H, DKV, DC])
    wqr_h = dt("wqr", [H, DKV, DR])
    wuk_h = dt("wuk", [H, DKV, DC])
    wuv_h = dt("wuv", [H, DKV, DH])
    wor_h = dt("wor", [4 * 128, 16 * 512])
    gwt_h = dt("gwt", [NI * 128, E])
    uwt_h = dt("uwt", [NI * 128, E])
    dwt_h = dt("dwt", [11 * 128, 4 * E])
    gs_h = dt("gs", [128, NI])
    us_h = dt("us", [128, NI])
    dsb_h = dt("dsb", [128, E])
    ident_h = dt("ident", [128, 128], BF16)
    bmask_h = dt("bmask", [128, 4 * 512], BF16)
    ang_h = dt("ang", [64, S])
    out_h = dt("out", [S, E], F32, kind="ExternalOutput")
    part_h = dt("part", [S, E], F32, kind="Internal")
    if _DEBUG_CANARY:
        canary_h = dt("canary", [128, 8], F32, kind="ExternalOutput")
    dbg_h = {}
    if _DEBUG_DUMPS:
        if _DEBUG_ATT:
            dbg_h["d_kb"] = dt("d_kb", [128, S], BF16, kind="ExternalOutput")
            dbg_h["d_vb"] = dt("d_vb", [128, S], BF16, kind="ExternalOutput")
            dbg_h["d_qb"] = dt("d_qb", [128, S], BF16, kind="ExternalOutput")
            dbg_h["d_pb"] = dt("d_pb", [128, S], BF16, kind="ExternalOutput")
            dbg_h["d_ssc"] = dt("d_ssc", [128, S], F32, kind="ExternalOutput")
            dbg_h["d_ptb"] = dt("d_ptb", [128, 16 * 512], BF16, kind="ExternalOutput")
            dbg_h["d_ctx"] = dt("d_ctx", [128, S], BF16, kind="ExternalOutput")
            dbg_h["d_smx"] = dt("d_smx", [128, 8], F32, kind="ExternalOutput")
        if _DEBUG_WO:
            dbg_h["d_ctxall"] = dt("d_ctxall", [128, H * S], BF16, kind="ExternalOutput")
            dbg_h["d_rb"] = dt("d_rb", [128, 512], F32, kind="ExternalOutput")
            dbg_h["d_ob"] = dt("d_ob", [128, 512], F32, kind="ExternalOutput")
        if _DEBUG_TILE0:
            dbg_h["d_xt0"] = dt("d_xt0", [128, E], F32, kind="ExternalOutput")
            dbg_h["d_xn0"] = dt("d_xn0", [128, E], BF16, kind="ExternalOutput")
            dbg_h["d_st0"] = dt("d_st0", [128, 64], F32, kind="ExternalOutput")
        for nm, shape, dtp in [("d_sin", [64, S], BF16), ("d_cos", [64, S], BF16),
                               ("d_xnT", [128, NE * HALF], BF16), ("d_ckvT", [128, S], BF16),
                               ("d_xdqT", [128, S], BF16), ("d_krT", [64, S], BF16)]:
            dbg_h[nm] = dt(nm, shape, dtp, kind="ExternalOutput")

    es = ExitStack()
    SBt = lambda name, shape, dtp=F32: es.enter_context(nc.sbuf_tensor(name, shape, dtp))

    ident = SBt("identsb", [128, 128], BF16)
    bmask = SBt("bmask_sb", [128, 4 * 512], BF16)
    sinT = SBt("sinT", [64, S], BF16)
    cosT = SBt("cosT", [64, S], BF16)
    scr_a = SBt("scr_a", [64, 512])
    scr_b = SBt("scr_b", [64, 512])
    zcol = SBt("zcol", [128, 128], BF16)
    w_qc = SBt("w_qc", [128, H * DC], BF16)
    w_qr = SBt("w_qr", [128, H * DR], BF16)
    w_uk = SBt("w_uk", [128, H * DC], BF16)
    w_uv = SBt("w_uv", [128, H * DH], BF16)
    gs_sb = SBt("gs_sb", [128, NI])
    us_sb = SBt("us_sb", [128, NI])
    c_kvT = SBt("c_kvT", [128, S], BF16)
    x_dqT = SBt("x_dqT", [128, S], BF16)
    krT = SBt("krT", [64, S], BF16)
    stats = SBt("stats", [128, 64])

    ph = ExitStack()
    SBp = lambda name, shape, dtp=F32: ph.enter_context(nc.sbuf_tensor(name, shape, dtp))
    angb = SBp("angb", [64, S])
    scr_c = SBp("scr_c", [64, 512])
    scr_i = SBp("scr_i", [64, 512], I32)
    w_dkv = SBp("w_dkv", [128, NE * DKV], BF16)
    w_dq = SBp("w_dq", [128, NE * DKV], BF16)
    w_kr = SBp("w_kr", [128, NE * DR], BF16)
    dsb = SBp("dsb_sb", [128, E])
    xnT = SBp("xnT", [128, NE * HALF], BF16)
    acc = SBp("acc", [128, 8 * E], BF16)
    xt0 = SBp("xt0", [128, E])
    xt1 = SBp("xt1", [128, E])
    xnat0 = SBp("xnat0", [128, E], BF16)
    xnat1 = SBp("xnat1", [128, E], BF16)
    gwt = [SBp(f"gwt{i}", [128, E], BF16) for i in range(2)]
    uwt = [SBp(f"uwt{i}", [128, E], BF16) for i in range(2)]
    hdnT = [SBp(f"hdnT{i}", [128, HALF], BF16) for i in range(4)]
    dwt = [SBp("dwt0", [128, 4 * E], BF16)]
    silu = SBp("silu", [128, 1024])
    rtmp = SBp("rtmp", [128, 1024])

    p_tr = ph.enter_context(nc.psum_tensor("p_tr", [128, 1024], F32))
    p_g = ph.enter_context(nc.psum_tensor("p_g", [128, 1024], F32))
    p_u = ph.enter_context(nc.psum_tensor("p_u", [128, 1024], F32))
    p_d0 = ph.enter_context(nc.psum_tensor("p_d0", [128, 512], F32))
    p_d1 = ph.enter_context(nc.psum_tensor("p_d1", [128, 512], F32))

    sem_es = ExitStack()
    mk_sem = lambda name: Sem(sem_es.enter_context(nc.semaphore(name)), name)
    s_pe = mk_sem("s_pe")
    s_dv = mk_sem("s_dv")
    s_sc = mk_sem("s_sc")
    s_cs = mk_sem("s_cs")
    s_cg = mk_sem("s_cg")
    s_x = [mk_sem("s_x0"), mk_sem("s_x1")]
    s_gw = [mk_sem("s_gw0"), mk_sem("s_gw1")]
    s_uw = [mk_sem("s_uw0"), mk_sem("s_uw1")]
    s_dw = [mk_sem("s_dw0"), mk_sem("s_dw1")]
    s_wo = [mk_sem("s_wo0"), mk_sem("s_wo1")]
    s_rb = [mk_sem("s_rb0"), mk_sem("s_rb1")]
    s_ob = [mk_sem("s_ob0"), mk_sem("s_ob1")]
    s_st = mk_sem("s_st")
    all_dma_sems = [s_cs, s_cg, s_x[0], s_x[1], s_gw[0], s_gw[1], s_uw[0], s_uw[1],
                    s_dw[0], s_dw[1], s_wo[0], s_wo[1], s_rb[0], s_rb[1], s_ob[0], s_ob[1], s_st]

    PE = Eng("pe", s_pe)
    DV = Eng("dv", s_dv)
    SC = Eng("sc", s_sc)
    SY = Eng("sy", s_cs)
    GP = Eng("gp", s_cg)

    def dma(eng, sem, src_ap, dst_ap, rd_bufs=(), wr_bufs=()):
        dep_read(eng, *rd_bufs)
        dep_write(eng, *wr_bufs)
        sem.n += 16
        h, val = sem.h, sem.n
        eng.trace.append(("i", sem.name, 16))
        eng.ops.append(lambda e: e.dma_start(out=dst_ap, in_=src_ap).then_inc(h, 16))
        mark = (sem, val)
        note_write(mark, *wr_bufs)
        note_read(mark, *rd_bufs)
        return mark

    def run(eng, fns, rd=(), wr=(), k=1):
        dep_read(eng, *rd)
        dep_write(eng, *wr)
        for fn in fns[:-1]:
            eng.op(fn)
        mark = eng.op_inc(fns[-1], k)
        note_write(mark, *wr)
        note_read(mark, *rd)
        return mark

    B = lambda n: Buf(n)
    b_const_sy = B("const_sy")
    b_const_gp = B("const_gp")
    b_sin, b_cos = B("sin"), B("cos")
    b_scr = B("scr")
    b_ckvT, b_xdqT, b_krT = B("ckvT"), B("xdqT"), B("krT")
    b_stats = B("stats")
    b_xnT = B("xnT")
    b_acc = B("acc")
    b_xt = [B("xt0"), B("xt1")]
    b_xnat = [B("xnat0"), B("xnat1")]
    b_gwt = [B("gwt0"), B("gwt1")]
    b_uwt = [B("uwt0"), B("uwt1")]
    b_hdnT = [B(f"hdnT{i}") for i in range(4)]
    b_dwt = [B("dwt0")]
    b_silu = B("silu")
    b_rtmp = B("rtmp")
    b_ptr = [B(f"p_tr{j}") for j in range(2)]
    b_pg, b_pu = B("p_g"), B("p_u")
    b_pd = [B("p_d0"), B("p_d1")]

    dma(SY, s_cs, ident_h, ident[:], wr_bufs=[b_const_sy])
    dma(SY, s_cs, bmask_h, bmask[:], wr_bufs=[b_const_sy])
    dma(SY, s_cs, ang_h, angb[:], wr_bufs=[b_const_sy])
    dma(SY, s_cs, gs_h, gs_sb[:], wr_bufs=[b_const_sy])
    dma(SY, s_cs, us_h, us_sb[:], wr_bufs=[b_const_sy])
    dma(SY, s_cs, dsb_h, dsb[:], wr_bufs=[b_const_sy])
    dma(GP, s_gw[0], gwt_h[0:128, :], gwt[0][:], wr_bufs=[b_gwt[0]])
    dma(GP, s_uw[0], uwt_h[0:128, :], uwt[0][:], wr_bufs=[b_uwt[0]])
    dma(GP, s_dw[0], dwt_h[0:128, :], dwt[0][:], wr_bufs=[b_dwt[0]])
    dma(GP, s_gw[1], gwt_h[128:256, :], gwt[1][:], wr_bufs=[b_gwt[1]])
    dma(GP, s_uw[1], uwt_h[128:256, :], uwt[1][:], wr_bufs=[b_uwt[1]])
    for ec in range(NE):
        dma(GP, s_cg, wdkv_h[ec * 128:(ec + 1) * 128, :], w_dkv[:, ec * DKV:(ec + 1) * DKV], wr_bufs=[b_const_gp])
        dma(GP, s_cg, wdq_h[ec * 128:(ec + 1) * 128, :], w_dq[:, ec * DKV:(ec + 1) * DKV], wr_bufs=[b_const_gp])
        dma(GP, s_cg, wkr_h[ec * 128:(ec + 1) * 128, :], w_kr[:, ec * DR:(ec + 1) * DR], wr_bufs=[b_const_gp])
    b_const_att = B("const_att")

    def emit_att_const_dmas():
        for h in range(H):
            dma(GP, s_cg, wqc_h[h], w_qc[:, h * DC:(h + 1) * DC], wr_bufs=[b_const_att])
            dma(GP, s_cg, wqr_h[h], w_qr[:, h * DR:(h + 1) * DR], wr_bufs=[b_const_att])
            dma(GP, s_cg, wuk_h[h], w_uk[:, h * DC:(h + 1) * DC], wr_bufs=[b_const_att])
            dma(GP, s_cg, wuv_h[h], w_uv[:, h * DH:(h + 1) * DH], wr_bufs=[b_const_att])

    run(DV, [lambda e: e.memset(stats[:, 42:43], EPS)], wr=[b_stats])
    b_zcol = B("zcol")
    run(DV, [lambda e: e.memset(zcol[:], 0.0)], wr=[b_zcol])
    for _ in range(_NOPS):
        run(DV, [lambda e: e.memset(stats[:, 43:44], 0.0)], wr=[b_stats])

    def emit_rope_tables():
        for blk in range(S // 512):
            sl = slice(blk * 512, blk * 512 + 512)
            for is_cos in (False, True):
                base = 0.75 if is_cos else 0.5
                tgt, b_tgt = (cosT, b_cos) if is_cos else (sinT, b_sin)
                fns = [
                    lambda e, sl=sl, base=base: e.tensor_scalar(scr_a[:, 0:512], angb[:, sl], INV2PI, base, ALU.mult, ALU.add),
                    lambda e: e.tensor_copy(scr_i[:, 0:512], scr_a[:, 0:512]),
                    lambda e: e.tensor_copy(scr_b[:, 0:512], scr_i[:, 0:512]),
                    lambda e, sl=sl: e.scalar_tensor_tensor(out=scr_a[:, 0:512], in0=scr_b[:, 0:512], scalar=-C1, in1=angb[:, sl], op0=ALU.mult, op1=ALU.add),
                    lambda e: e.scalar_tensor_tensor(out=scr_c[:, 0:512], in0=scr_b[:, 0:512], scalar=-C2, in1=scr_a[:, 0:512], op0=ALU.mult, op1=ALU.add),
                    lambda e: e.scalar_tensor_tensor(out=scr_a[:, 0:512], in0=scr_b[:, 0:512], scalar=-C3, in1=scr_c[:, 0:512], op0=ALU.mult, op1=ALU.add),
                ]
                if is_cos:
                    fns.append(lambda e: e.tensor_scalar(scr_a[:, 0:512], scr_a[:, 0:512], A1, A2, ALU.add, ALU.add))
                fns += [
                    lambda e: e.tensor_scalar(scr_b[:, 0:512], scr_a[:, 0:512], -PI_F, None, ALU.is_lt),
                    lambda e: e.tensor_scalar(scr_c[:, 0:512], scr_a[:, 0:512], PI_F, None, ALU.is_gt),
                    lambda e: e.tensor_sub(scr_b[:, 0:512], scr_b[:, 0:512], scr_c[:, 0:512]),
                    lambda e: e.scalar_tensor_tensor(out=scr_a[:, 0:512], in0=scr_b[:, 0:512], scalar=TWO_PI, in1=scr_a[:, 0:512], op0=ALU.mult, op1=ALU.add),
                    lambda e: e.tensor_scalar(scr_a[:, 0:512], scr_a[:, 0:512], PI_DN, -PI_DN, ALU.min, ALU.max),
                ]
                run(DV, fns, rd=[b_const_sy], wr=[b_scr])
                run(SC, [
                    lambda e, tgt=tgt, sl=sl: e.activation(tgt[:, sl], scr_a[:, 0:512], ACT.Sin),
                ], rd=[b_scr], wr=[b_tgt])

    def dump(nm, sb_ap, rd):
        if _DEBUG_DUMPS:
            dma(SY, s_st, sb_ap, dbg_h[nm].tensor.ap() if hasattr(dbg_h[nm], "tensor") else dbg_h[nm], rd_bufs=[rd])

    def apply_rope(src, src_buf, src_col0, dst, dst_buf, t0, width, dst_row0=0, eng=None):
        eng = eng or DV
        for b0 in range(0, width, 512):
            w = min(512, width - b0)
            cs = slice(t0 + b0, t0 + b0 + w)
            ss = slice(src_col0 + b0, src_col0 + b0 + w)
            run(eng, [
                lambda e, ss=ss, w=w, cs=cs: e.tensor_mul(scr_a[0:32, 0:w], src[0:32, ss], cosT[0:32, cs]),
                lambda e, ss=ss, w=w, cs=cs: e.tensor_mul(scr_b[0:32, 0:w], src[32:64, ss], sinT[32:64, cs]),
                lambda e, w=w, cs=cs: e.tensor_sub(dst[dst_row0:dst_row0 + 32, cs], scr_a[0:32, 0:w], scr_b[0:32, 0:w]),
                lambda e, ss=ss, w=w, cs=cs: e.tensor_mul(scr_a[0:32, 0:w], src[0:32, ss], sinT[0:32, cs]),
                lambda e, ss=ss, w=w, cs=cs: e.tensor_mul(scr_b[0:32, 0:w], src[32:64, ss], cosT[32:64, cs]),
                lambda e, w=w, cs=cs: e.tensor_add(dst[dst_row0 + 32:dst_row0 + 64, cs], scr_a[0:32, 0:w], scr_b[0:32, 0:w]),
            ], rd=[src_buf, b_sin, b_cos], wr=[dst_buf, b_scr])

    def emit_residual_tile(half, tc, eng):
        toff = half * HALF
        xt, b_x = (xt0, xt1)[tc % 2], b_xt[tc % 2]
        tmp, b_t = (rtmp, b_rtmp) if eng is GP else (silu, b_silu)
        dma(SY, s_x[tc % 2], x_h[toff + tc * 128: toff + (tc + 1) * 128, :], xt[:], wr_bufs=[b_x])
        for q4 in range(2):
            sl = slice(q4 * 1024, q4 * 1024 + 1024)
            run(eng, [
                lambda e, tc=tc, sl=sl, tmp=tmp: e.tensor_mul(
                    tmp[:], acc[:, tc * E + sl.start: tc * E + sl.stop], dsb[:, sl]),
            ], rd=[b_acc, b_const_sy], wr=[b_t])
            run(eng, [
                lambda e, xt=xt, sl=sl, tmp=tmp: e.tensor_add(xt[:, sl], xt[:, sl], tmp[:]),
            ], rd=[b_t], wr=[b_x])
        dma(SY, s_x[tc % 2], xt[:], part_h[toff + tc * 128: toff + (tc + 1) * 128, :], rd_bufs=[b_x])

    for half in range(2):
        toff = half * HALF
        for i in range(8):
            xt = (xt0, xt1)[i % 2]
            b_x = b_xt[i % 2]
            xn_t = (xnat0, xnat1)[i % 2]
            b_xn = b_xnat[i % 2]
            dma(SY, s_x[i % 2], x_h[toff + i * 128: toff + (i + 1) * 128, :], xt[:], wr_bufs=[b_x])
            ms_ap = bass.AP(stats, 1, [[64, 128], [3, 8]])
            cv_ap = bass.AP(stats, 2, [[64, 128], [3, 8]])
            run(DV, [
                (lambda e, xt=xt, j=j: e.bn_stats(stats[:, 6 * j:6 * j + 6], xt[:, 512 * j:512 * j + 512]))
                for j in range(4)
            ], rd=[b_x], wr=[b_stats])
            DV.drain()
            run(DV, [
                lambda e, ms_ap=ms_ap: e.reduce_sum(stats[:, 24:25], ms_ap, AX.X),
                lambda e, cv_ap=cv_ap: e.reduce_sum(stats[:, 25:26], cv_ap, AX.X),
                lambda e, ms_ap=ms_ap: e.tensor_mul(stats[:, 32:40], ms_ap, ms_ap),
            ], rd=[b_stats], wr=[b_stats])
            DV.drain()
            run(DV, [
                lambda e: e.tensor_scalar(stats[:, 27:28], stats[:, 24:25], -0.125, None, ALU.mult),
                lambda e: e.tensor_scalar(stats[:, 28:29], stats[:, 25:26], 1.0 / 2048.0, None, ALU.mult),
                lambda e: e.reduce_sum(stats[:, 26:27], stats[:, 32:40], AX.X),
            ], rd=[b_stats], wr=[b_stats])
            DV.drain()
            run(DV, [
                lambda e: e.tensor_scalar(stats[:, 30:31], stats[:, 27:28], stats[:, 27:28], None, ALU.mult),
                lambda e: e.scalar_tensor_tensor(out=stats[:, 29:30], in0=stats[:, 26:27], scalar=0.125, in1=stats[:, 28:29], op0=ALU.mult, op1=ALU.add),
            ], rd=[b_stats], wr=[b_stats])
            DV.drain()
            run(DV, [
                lambda e: e.tensor_sub(stats[:, 31:32], stats[:, 29:30], stats[:, 30:31]),
            ], rd=[b_stats], wr=[b_stats])
            run(SC, [
                lambda e: e.activation(stats[:, 41:42], stats[:, 31:32], ACT.Sqrt, bias=stats[:, 42:43]),
            ], rd=[b_stats], wr=[b_stats])
            run(DV, [
                lambda e: e.reciprocal(stats[:, 40:41], stats[:, 41:42]),
            ], rd=[b_stats], wr=[b_stats])
            DV.drain()
            run(DV, [
                lambda e, xt=xt, xn_t=xn_t: e.tensor_scalar(xn_t[:], xt[:], stats[:, 27:28], stats[:, 40:41], ALU.add, ALU.mult),
            ], rd=[b_x, b_stats], wr=[b_xn])
            for hp in range(2):
                for bank in range(2):
                    last = (bank == 1)
                    fns = [
                        (lambda e, xn_t=xn_t, ec=8 * hp + 4 * bank + q, bank=bank, q=q, last=last:
                         e.matmul(p_tr[:, bank * 512 + q * 128: bank * 512 + (q + 1) * 128],
                                  xn_t[:, ec * 128:(ec + 1) * 128], ident[:],
                                  start=(q == 0), stop=(q == 3 and not last)))
                        for q in range(4)
                    ]
                    if last:
                        fns.append(lambda e, bank=bank: e.matmul(
                            p_tr[:, bank * 512: bank * 512 + 128], zcol[:],
                            ident[:], start=False, stop=True))
                    run(PE, fns, rd=[b_xn, b_const_sy, b_zcol], wr=[b_ptr[bank]])
                run(SC, [
                    lambda e, i=i, hp=hp: e.copy(
                        bass.AP(xnT, 8 * hp * HALF + i * 128, [[NE * HALF, 128], [HALF, 8], [1, 128]]),
                        p_tr[:].rearrange("p (a b) -> p a b", b=128)),
                ], rd=[b_ptr[0], b_ptr[1]], wr=[b_xnT])
            if _DEBUG_TILE0 and half == 0 and i == 0:
                dump("d_xt0", xt[:], b_x)
                dump("d_xn0", xn_t[:], b_xn)
                dump("d_st0", stats[:], b_stats)

        if half == 0:
            emit_rope_tables()
        gl = slice(toff, toff + HALF)
        run(PE, [
            (lambda e, ec=ec, tb=tb: e.matmul(p_g[:, tb * 512:(tb + 1) * 512],
                                              w_dkv[:, ec * 128:(ec + 1) * 128],
                                              xnT[:, ec * HALF + tb * 512: ec * HALF + tb * 512 + 512],
                                              start=(ec == 0), stop=(ec == 15)))
            for tb in range(2) for ec in range(16)
        ], rd=[b_const_gp, b_xnT], wr=[b_pg])
        run(SC, [lambda e, gl=gl: e.copy(c_kvT[:, gl], p_g[:])], rd=[b_pg], wr=[b_ckvT])
        run(PE, [
            (lambda e, ec=ec, tb=tb: e.matmul(p_u[:, tb * 512:(tb + 1) * 512],
                                              w_dq[:, ec * 128:(ec + 1) * 128],
                                              xnT[:, ec * HALF + tb * 512: ec * HALF + tb * 512 + 512],
                                              start=(ec == 0), stop=(ec == 15)))
            for tb in range(2) for ec in range(16)
        ], rd=[b_const_gp, b_xnT], wr=[b_pu])
        run(SC, [lambda e, gl=gl: e.copy(x_dqT[:, gl], p_u[:])], rd=[b_pu], wr=[b_xdqT])
        run(PE, [
            (lambda e, ec=ec, tb=tb: e.matmul(p_g[0:64, tb * 512:(tb + 1) * 512],
                                              w_kr[:, ec * 64:(ec + 1) * 64],
                                              xnT[:, ec * HALF + tb * 512: ec * HALF + tb * 512 + 512],
                                              start=(ec == 0), stop=(ec == 15)))
            for tb in range(2) for ec in range(16)
        ], rd=[b_const_gp, b_xnT], wr=[b_pg])
        run(SC, [lambda e: e.copy(xt0[0:64, 0:HALF], p_g[0:64, :])], rd=[b_pg], wr=[b_xt[0]])
        apply_rope(xt0, b_xt[0], 0, krT, b_krT, toff, HALF)

        if _DEBUG_DUMPS and half == 0:
            dump("d_xnT", xnT[:], b_xnT)

        if half == 1:
            for tc in range(8):
                emit_residual_tile(0, tc, GP)

        for ic in range(NI):
            g = ic // 4
            sub = ic % 4
            gwc, bgw = gwt[ic % 2], b_gwt[ic % 2]
            uwc, buw = uwt[ic % 2], b_uwt[ic % 2]
            if ic >= 2:
                dma(GP, s_gw[ic % 2], gwt_h[ic * 128:(ic + 1) * 128, :], gwc[:], wr_bufs=[bgw])
                dma(GP, s_uw[ic % 2], uwt_h[ic * 128:(ic + 1) * 128, :], uwc[:], wr_bufs=[buw])
            if sub == 0 and g >= 1:
                dma(GP, s_dw[g % 2], dwt_h[g * 128:(g + 1) * 128, :], dwt[0][:],
                    wr_bufs=[b_dwt[0]])
            run(PE, [
                (lambda e, ec=ec, tb=tb, gwc=gwc: e.matmul(
                    p_g[:, tb * 512:(tb + 1) * 512], gwc[:, ec * 128:(ec + 1) * 128],
                    xnT[:, ec * HALF + tb * 512: ec * HALF + tb * 512 + 512],
                    start=(ec == 0), stop=(ec == 15)))
                for tb in range(2) for ec in range(16)
            ], rd=[bgw, b_xnT], wr=[b_pg])
            run(PE, [
                (lambda e, ec=ec, tb=tb, uwc=uwc: e.matmul(
                    p_u[:, tb * 512:(tb + 1) * 512], uwc[:, ec * 128:(ec + 1) * 128],
                    xnT[:, ec * HALF + tb * 512: ec * HALF + tb * 512 + 512],
                    start=(ec == 0), stop=(ec == 15)))
                for tb in range(2) for ec in range(16)
            ], rd=[buw, b_xnT], wr=[b_pu])
            run(SC, [
                lambda e, ic=ic: e.activation(silu[:], p_g[:], ACT.Silu, scale=gs_sb[:, ic:ic + 1]),
            ], rd=[b_pg, b_const_sy], wr=[b_silu])
            run(DV, [
                lambda e, ic=ic, sub=sub: e.scalar_tensor_tensor(
                    out=hdnT[sub][:], in0=p_u[:],
                    scalar=us_sb[:, ic:ic + 1], in1=silu[:], op0=ALU.mult, op1=ALU.mult),
            ], rd=[b_pu, b_silu, b_const_sy], wr=[b_hdnT[sub]])

            if sub == 3:
                dwc, bdw = dwt[0], b_dwt[0]
                for tc in range(8):
                    for hb in range(4):
                        k2 = (tc * 4 + hb) % 2
                        pd, bpd = (p_d0, p_d1)[k2], b_pd[k2]
                        run(PE, [
                            (lambda e, icq=icq, tc=tc, hb=hb, pd=pd, dwc=dwc:
                             e.matmul(pd[:], hdnT[icq][:, tc * 128:(tc + 1) * 128],
                                      dwc[:, icq * E + hb * 512: icq * E + hb * 512 + 512],
                                      start=(icq == 0), stop=(icq == 3)))
                            for icq in range(4)
                        ], rd=[b_hdnT[0], b_hdnT[1], b_hdnT[2], b_hdnT[3], bdw], wr=[bpd])
                        asl = slice(tc * E + hb * 512, tc * E + hb * 512 + 512)
                        if g == 0:
                            run(DV, [lambda e, asl=asl, pd=pd: e.tensor_copy(acc[:, asl], pd[:])],
                                rd=[bpd], wr=[b_acc])
                        else:
                            run(DV, [lambda e, asl=asl, pd=pd: e.scalar_tensor_tensor(
                                out=acc[:, asl], in0=pd[:], scalar=1.0, in1=acc[:, asl],
                                op0=ALU.mult, op1=ALU.add)], rd=[bpd], wr=[b_acc])

        if half == 0:
            dma(GP, s_gw[0], gwt_h[0:128, :], gwt[0][:], wr_bufs=[b_gwt[0]])
            dma(GP, s_uw[0], uwt_h[0:128, :], uwt[0][:], wr_bufs=[b_uwt[0]])
            dma(GP, s_dw[0], dwt_h[0:128, :], dwt[0][:], wr_bufs=[b_dwt[0]])
            dma(GP, s_gw[1], gwt_h[128:256, :], gwt[1][:], wr_bufs=[b_gwt[1]])
            dma(GP, s_uw[1], uwt_h[128:256, :], uwt[1][:], wr_bufs=[b_uwt[1]])
            emit_att_const_dmas()

    for tc in range(8):
        emit_residual_tile(1, tc, (DV, GP)[tc % 2])

    if _DEBUG_DUMPS:
        dump("d_sin", sinT[:], b_sin)
        dump("d_cos", cosT[:], b_cos)
        dump("d_ckvT", c_kvT[:], b_ckvT)
        dump("d_xdqT", x_dqT[:], b_xdqT)
        dump("d_krT", krT[:], b_krT)

    ph.close()
    marks = [(s_pe, s_pe.n), (s_dv, s_dv.n), (s_sc, s_sc.n)] + \
            [(s, s.n) for s in all_dma_sems]
    for eng in (PE, DV, SC, SY, GP):
        for m in marks:
            eng.wait(*m)

    if _STOP_AFTER_FFN:
        for s_fin in (s_x[0], s_x[1], s_st):
            if s_fin.n > 0:
                SY.trace.append(("w", s_fin.name, s_fin.n))
                SY.ops.append(lambda e, h=s_fin.h, v=s_fin.n: e.wait_ge(h, v))
        build_nc.engines = dict(pe=PE, dv=DV, sc=SC, sy=SY, gp=GP)
        with es, sem_es, nc.Block() as block:
            @block.tensor
            def _(t):
                for f in PE.ops:
                    f(t)

            @block.vector
            def _(v):
                for f in DV.ops:
                    f(v)

            @block.scalar
            def _(s):
                for f in SC.ops:
                    f(s)

            @block.sync
            def _(s):
                for f in SY.ops:
                    f(s)

            @block.gpsimd
            def _(g):
                for f in GP.ops:
                    f(g)
        return nc

    at = ExitStack()
    SBa = lambda name, shape, dtp=F32: at.enter_context(nc.sbuf_tensor(name, shape, dtp))
    ctx_all = SBa("ctx_all", [128, H * S], BF16)
    kT = [SBa("kT0", [128, S], BF16), SBa("kT1", [128, S], BF16)]
    vb = [SBa("vb0", [128, S], BF16), SBa("vb1", [128, S], BF16)]
    qu = [SBa("qu0", [128, S], BF16), SBa("qu1", [128, S], BF16)]
    qrp = SBa("qrp", [64, S])
    ptb = [SBa("ptb0", [128, 16 * 512], BF16), SBa("ptb1", [128, 16 * 512], BF16)]
    Rb = [SBa("Rb0", [128, 512]), SBa("Rb1", [128, 512])]
    ones_sb = SBa("ones_sb", [128, 128], BF16)
    wob = SBa("wob", [128, 16 * 512], BF16)
    obuf = [SBa(f"ob{i}", [128, 512]) for i in range(4)]
    rbuf = [SBa(f"rb{i}", [128, 512]) for i in range(4)]

    p_s = at.enter_context(nc.psum_tensor("p_s", [128, 2048], F32))
    p_t1 = at.enter_context(nc.psum_tensor("p_t1", [128, 512], F32))
    p_t2 = at.enter_context(nc.psum_tensor("p_t2", [128, 512], F32))
    p_c1 = at.enter_context(nc.psum_tensor("p_c1", [128, 512], F32))
    p_c2 = at.enter_context(nc.psum_tensor("p_c2", [128, 512], F32))

    b_ctx = B("ctx")
    b_kT = [B("kT0"), B("kT1")]
    b_vb = [B("vb0"), B("vb1")]
    b_qu = [B("qu0"), B("qu1")]
    b_qrp = B("qrp")
    b_ptb = [B("ptb0"), B("ptb1")]
    b_R = [B("Rb0"), B("Rb1")]
    b_ones = B("ones_sb")
    b_wob = [B("wob0"), B("wob1")]
    b_ob = [B(f"ob{i}") for i in range(4)]
    b_rb = [B(f"rb{i}") for i in range(4)]
    b_ps = [B(f"p_s{j}") for j in range(4)]
    b_pt1, b_pt2, b_pc1, b_pc2 = B("p_t1"), B("p_t2"), B("p_c1"), B("p_c2")

    run(DV, [lambda e: e.memset(ones_sb[:], 1.0)], wr=[b_ones])
    for j in range(2):
        run(SC, [lambda e, j=j: e.copy(kT[j][64:128, :], krT[:])], rd=[b_krT], wr=[b_kT[j]])

    def emit_proj(h):
        kb, b_kb = kT[h % 2], b_kT[h % 2]
        vv, b_vv = vb[h % 2], b_vb[h % 2]
        qb, b_qb = qu[h % 2], b_qu[h % 2]
        for blk in range(4):
            pt, bpt = (p_t1, p_t2)[blk % 2], (b_pt1, b_pt2)[blk % 2]
            run(PE, [
                lambda e, h=h, blk=blk, pt=pt: e.matmul(
                    pt[0:64, :], w_uk[:, h * 64:(h + 1) * 64],
                    c_kvT[:, blk * 512:(blk + 1) * 512], start=True, stop=True),
            ], rd=[b_const_att, b_ckvT], wr=[bpt])
            run(SC, [
                lambda e, blk=blk, kb=kb, pt=pt: e.copy(kb[0:64, blk * 512:(blk + 1) * 512], pt[0:64, :]),
            ], rd=[bpt], wr=[b_kb])
        for blk4 in range(4):
            pc, bpc = (p_c1, p_c2)[blk4 % 2], (b_pc1, b_pc2)[blk4 % 2]
            run(PE, [
                (lambda e, h=h, j=j, blk4=blk4, pc=pc: e.matmul(
                    pc[:, j * 128:(j + 1) * 128],
                    c_kvT[:, (blk4 * 4 + j) * 128:(blk4 * 4 + j + 1) * 128],
                    w_uv[:, h * 128:(h + 1) * 128], start=True, stop=False))
                for j in range(4)
            ] + [
                lambda e, pc=pc: e.matmul(pc[:, 0:128], zcol[:], ident[:],
                                          start=False, stop=True),
            ], rd=[b_const_att, b_ckvT, b_const_sy, b_zcol], wr=[bpc])
            run(SC, [
                lambda e, blk4=blk4, vv=vv, pc=pc: e.copy(vv[:, blk4 * 512:(blk4 + 1) * 512], pc[:, 0:512]),
            ], rd=[bpc], wr=[b_vv])
        for blk in range(4):
            pt, bpt = (p_t1, p_t2)[blk % 2], (b_pt1, b_pt2)[blk % 2]
            run(PE, [
                lambda e, h=h, blk=blk, pt=pt: e.matmul(
                    pt[0:64, :], w_qc[:, h * 64:(h + 1) * 64],
                    x_dqT[:, blk * 512:(blk + 1) * 512], start=True, stop=True),
            ], rd=[b_const_att, b_xdqT], wr=[bpt])
            run(SC, [
                lambda e, blk=blk, qb=qb, pt=pt: e.copy(qb[0:64, blk * 512:(blk + 1) * 512], pt[0:64, :]),
            ], rd=[bpt], wr=[b_qb])
        for blk in range(4):
            pt, bpt = (p_t1, p_t2)[blk % 2], (b_pt1, b_pt2)[blk % 2]
            run(PE, [
                lambda e, h=h, blk=blk, pt=pt: e.matmul(
                    pt[0:64, :], w_qr[:, h * 64:(h + 1) * 64],
                    x_dqT[:, blk * 512:(blk + 1) * 512], start=True, stop=True),
            ], rd=[b_const_att, b_xdqT], wr=[bpt])
            run(SC, [
                lambda e, blk=blk, pt=pt: e.copy(qrp[0:64, blk * 512:(blk + 1) * 512], pt[0:64, :]),
            ], rd=[bpt], wr=[b_qrp])
        apply_rope(qrp, b_qrp, 0, qb, b_qb, 0, S, dst_row0=64, eng=GP)

    def emit_scores(qblk, kb, b_kb, qb, b_qb):
        nkc = 4 * qblk + 4
        pt_par, b_pt_par = ptb[qblk % 2], b_ptb[qblk % 2]
        for kc in range(nkc):
            o = kc - 4 * qblk
            c0 = o * 128 if o > 0 else 0
            psl = slice((kc % 4) * 512 + c0, (kc % 4) * 512 + 512)
            qsl = slice(qblk * 512 + c0, qblk * 512 + 512)
            tsl = slice(kc * 512 + c0, kc * 512 + 512)
            run(PE, [
                lambda e, psl=psl, qsl=qsl, kc=kc, kb=kb, qb=qb: e.matmul(
                    p_s[:, psl], kb[:, kc * 128:(kc + 1) * 128], qb[:, qsl],
                    start=True, stop=True),
            ], rd=[b_qb, b_kb], wr=[b_ps[kc % 4]])
            run(SC, [
                lambda e, tsl=tsl, psl=psl, pt_par=pt_par: e.activation(
                    pt_par[:, tsl], p_s[:, psl], ACT.Exp, scale=INV_SQRT_DH),
            ], rd=[b_ps[kc % 4]], wr=[b_pt_par])
        fns = []
        for o in range(4):
            kc = 4 * qblk + o
            if o > 0:
                fns.append(lambda e, kc=kc, o=o, pt_par=pt_par: e.memset(
                    pt_par[:, kc * 512: kc * 512 + o * 128], 0.0))
            fns.append(lambda e, kc=kc, o=o, pt_par=pt_par: e.tensor_mul(
                pt_par[:, kc * 512 + o * 128: kc * 512 + (o + 1) * 128],
                pt_par[:, kc * 512 + o * 128: kc * 512 + (o + 1) * 128],
                bmask[:, o * 512 + o * 128: o * 512 + (o + 1) * 128]))
        run(DV, fns, rd=[b_const_sy], wr=[b_pt_par])

    def emit_sums(qblk):
        nkc = 4 * qblk + 4
        pt_par, b_pt_par = ptb[qblk % 2], b_ptb[qblk % 2]
        psum_t, b_psum = (p_t1, p_t2)[qblk % 2], (b_pt1, b_pt2)[qblk % 2]
        run(PE, [
            (lambda e, kc=kc, psum_t=psum_t, pt_par=pt_par, nkc=nkc: e.matmul(
                psum_t[:], ones_sb[:], pt_par[:, kc * 512:(kc + 1) * 512],
                start=(kc == 0), stop=(kc == nkc - 1)))
            for kc in range(nkc)
        ], rd=[b_pt_par, b_ones], wr=[b_psum])
        run(DV, [
            lambda e, qblk=qblk, psum_t=psum_t: e.reciprocal(Rb[qblk % 2][:], psum_t[:]),
        ], rd=[b_psum], wr=[b_R[qblk % 2]])

    def emit_ctx(h, qblk, vv, b_vv):
        nkc = 4 * qblk + 4
        pt_par, b_pt_par = ptb[qblk % 2], b_ptb[qblk % 2]
        pc, bpc = (p_c1, p_c2)[qblk % 2], (b_pc1, b_pc2)[qblk % 2]
        run(PE, [
            (lambda e, kc=kc, vv=vv, pc=pc, nkc=nkc, pt_par=pt_par:
             e.matmul(pc[:], vv[:, kc * 128:(kc + 1) * 128],
                      pt_par[:, kc * 512:(kc + 1) * 512], start=(kc == 0), stop=(kc == nkc - 1)))
            for kc in range(nkc)
        ], rd=[b_vv, b_pt_par], wr=[bpc])
        run(DV, [
            lambda e, h=h, qblk=qblk, pc=pc: e.tensor_mul(
                ctx_all[:, h * S + qblk * 512: h * S + (qblk + 1) * 512],
                pc[:], Rb[qblk % 2][:]),
        ], rd=[bpc, b_R[qblk % 2]], wr=[b_ctx])

    emit_proj(0)
    for h in range(H):
        kb, b_kb = kT[h % 2], b_kT[h % 2]
        vv, b_vv = vb[h % 2], b_vb[h % 2]
        qb, b_qb = qu[h % 2], b_qu[h % 2]
        if h + 1 < H:
            emit_proj(h + 1)
        emit_scores(0, kb, b_kb, qb, b_qb)
        emit_scores(1, kb, b_kb, qb, b_qb)
        emit_sums(0)
        emit_ctx(h, 0, vv, b_vv)
        emit_scores(2, kb, b_kb, qb, b_qb)
        emit_sums(1)
        emit_ctx(h, 1, vv, b_vv)
        emit_scores(3, kb, b_kb, qb, b_qb)
        emit_sums(2)
        emit_ctx(h, 2, vv, b_vv)
        emit_sums(3)
        emit_ctx(h, 3, vv, b_vv)

    if _DEBUG_ATT:
        dump("d_ctx", ctx_all[:, 0:S], b_ctx)
    if _DEBUG_WO:
        dump("d_ctxall", ctx_all[:], b_ctx)

    b_po = [b_ps[0], b_ps[1], b_ps[2], b_ps[3], b_pt1, b_pt2, b_pc1, b_pc2]
    po_t = [p_s[:, 0:512], p_s[:, 512:1024], p_s[:, 1024:1536], p_s[:, 1536:2048],
            p_t1[:], p_t2[:], p_c1[:], p_c2[:]]
    for eb in range(4):
        for hf in range(2):
            dma(GP, s_wo[hf], wor_h[eb * 128:(eb + 1) * 128, hf * 4096:(hf + 1) * 4096],
                wob[:, hf * 4096:(hf + 1) * 4096], wr_bufs=[b_wob[hf]])
        for grp in range(2):
            for tc8 in range(4):
                tc = grp * 8 + tc8
                dma(SY, s_rb[tc8 % 2], part_h[tc * 128:(tc + 1) * 128, eb * 512:(eb + 1) * 512],
                    rbuf[tc8 % 4][:], wr_bufs=[b_rb[tc8 % 4]])
            for h in range(H):
                run(PE, [
                    (lambda e, h=h, tc8=tc8, grp=grp:
                     e.matmul(po_t[tc8],
                              ctx_all[:, h * S + (grp * 8 + tc8) * 128: h * S + (grp * 8 + tc8) * 128 + 128],
                              wob[:, h * 512:(h + 1) * 512], start=(h == 0), stop=(h == 15)))
                    for tc8 in range(8)
                ], rd=[b_ctx, b_wob[h // 8]], wr=[b_po[j] for j in range(8)])
            for tc8 in range(8):
                tc = grp * 8 + tc8
                rb, b_r = rbuf[tc8 % 4], b_rb[tc8 % 4]
                ob, b_o = obuf[tc8 % 4], b_ob[tc8 % 4]
                run(DV, [
                    lambda e, tc8=tc8, rb=rb, ob=ob: e.scalar_tensor_tensor(
                        out=ob[:], in0=po_t[tc8], scalar=1.0, in1=rb[:],
                        op0=ALU.mult, op1=ALU.add),
                ], rd=[b_r, b_po[tc8]], wr=[b_o])
                dma(SC, s_ob[tc8 % 2], ob[:], out_h[tc * 128:(tc + 1) * 128, eb * 512:(eb + 1) * 512], rd_bufs=[b_o])
                if tc8 < 4:
                    tcn = grp * 8 + tc8 + 4
                    dma(SY, s_rb[tc8 % 2], part_h[tcn * 128:(tcn + 1) * 128, eb * 512:(eb + 1) * 512],
                        rbuf[tc8 % 4][:], wr_bufs=[b_rb[tc8 % 4]])

    for s_fin in (s_x[0], s_x[1], s_ob[0], s_ob[1], s_st):
        if s_fin.n > 0:
            SY.trace.append(("w", s_fin.name, s_fin.n))
            SY.ops.append(lambda e, h=s_fin.h, v=s_fin.n: e.wait_ge(h, v))

    build_nc.engines = dict(pe=PE, dv=DV, sc=SC, sy=SY, gp=GP)

    with es, at, sem_es, nc.Block() as block:
        @block.tensor
        def _(t):
            for f in PE.ops:
                f(t)

        @block.vector
        def _(v):
            for f in DV.ops:
                f(v)

        @block.scalar
        def _(s):
            for f in SC.ops:
                f(s)

        @block.sync
        def _(s):
            for f in SY.ops:
                f(s)

        @block.gpsimd
        def _(g):
            for f in GP.ops:
                f(g)

    return nc


def make_feed(L0_attn_x, L0_attn_W_dkv, L0_attn_W_dq, L0_attn_W_kr, L0_attn_W_qc,
              L0_attn_W_qr, L0_attn_W_uk, L0_attn_W_uv, L0_attn_W_o,
              L0_ffn_gate_w, L0_ffn_up_w, L0_ffn_down_w, L0_ffn_gate_s,
              L0_ffn_up_s, L0_ffn_down_s):
    f32 = np.float32
    perm = np.concatenate([np.arange(0, 64, 2), np.arange(1, 64, 2)])
    freqs = np.array(FREQ_BITS, dtype=np.uint32).view(np.float32)
    ang = (freqs[:, None] * np.arange(S, dtype=np.float32)[None, :]).astype(np.float32)
    ang = np.concatenate([ang, ang], axis=0)
    ident = np.eye(128, dtype=np.float32).astype(ml_dtypes.bfloat16)
    bmask = np.zeros((128, 4, 4, 128), dtype=np.float32)
    triu = np.triu(np.ones((128, 128), dtype=np.float32))
    for o in range(4):
        for qc in range(4):
            if qc > o:
                bmask[:, o, qc, :] = 1.0
            elif qc == o:
                bmask[:, o, qc, :] = triu
    bmask = np.ascontiguousarray(bmask.reshape(128, 2048)).astype(ml_dtypes.bfloat16)

    gwt = np.asarray(L0_ffn_gate_w, dtype=f32).reshape(NI, 128, NE, 128) \
        .transpose(0, 3, 2, 1).reshape(NI * 128, E)
    uwt = np.asarray(L0_ffn_up_w, dtype=f32).reshape(NI, 128, NE, 128) \
        .transpose(0, 3, 2, 1).reshape(NI * 128, E)
    dwt = np.ascontiguousarray(np.asarray(L0_ffn_down_w, dtype=f32).T) \
        .reshape(11, 4, 128, E).transpose(0, 2, 1, 3).reshape(11 * 128, 4 * E)
    shared = {
        "wdkv": np.ascontiguousarray(L0_attn_W_dkv, dtype=f32),
        "wdq": np.ascontiguousarray(L0_attn_W_dq, dtype=f32),
        "wkr": np.ascontiguousarray(np.asarray(L0_attn_W_kr, dtype=f32)[:, perm]),
        "wqc": np.ascontiguousarray(L0_attn_W_qc, dtype=f32),
        "wqr": np.ascontiguousarray(np.asarray(L0_attn_W_qr, dtype=f32)[:, :, perm]),
        "wuk": np.ascontiguousarray(L0_attn_W_uk, dtype=f32),
        "wuv": np.ascontiguousarray(L0_attn_W_uv, dtype=f32),
        "wor": np.ascontiguousarray(np.asarray(L0_attn_W_o, dtype=f32)
                                    .reshape(16, 128, 4, 512).transpose(2, 1, 0, 3)
                                    .reshape(4 * 128, 16 * 512)),
        "gwt": np.ascontiguousarray(gwt),
        "uwt": np.ascontiguousarray(uwt),
        "dwt": np.ascontiguousarray(dwt),
        "gs": np.ascontiguousarray(np.asarray(L0_ffn_gate_s, dtype=f32).reshape(NI, 128).T),
        "us": np.ascontiguousarray(np.asarray(L0_ffn_up_s, dtype=f32).reshape(NI, 128).T),
        "dsb": np.ascontiguousarray(np.broadcast_to(
            np.asarray(L0_ffn_down_s, dtype=f32)[None, :], (128, E))),
        "ident": ident,
        "bmask": bmask,
        "ang": ang,
    }
    xs = np.ascontiguousarray(np.asarray(L0_attn_x, dtype=f32))
    return shared, xs


def kernel(**inputs):
    shared, xs = make_feed(**inputs)
    in_maps = [dict(shared, x=xs[b]) for b in range(8)]
    nc = build_nc()
    res = run_bass_kernel_spmd(nc, in_maps, core_ids=list(range(8)))
    return np.stack([np.asarray(r["out"], dtype=np.float32) for r in res.results], axis=0)


# version 17
# speedup vs baseline: 1.0667x; 1.0667x over previous
import math
import numpy as np
import ml_dtypes
from contextlib import ExitStack

import concourse.bass as bass
import concourse.mybir as mybir
from concourse.bass_utils import run_bass_kernel_spmd

F32 = mybir.dt.float32
BF16 = mybir.dt.bfloat16
I32 = mybir.dt.int32
AX = mybir.AxisListType
ALU = mybir.AluOpType
ACT = mybir.ActivationFunctionType

S = 2048
E = 2048
H = 16
DKV = 128
DC = 64
DR = 64
DH = 128
INTER = 5632
NI = INTER // 128
NE = E // 128
HALF = S // 2

INV_SQRT_DH = float(np.float32(1.0 / math.sqrt(DH)))
NEG_BIG = -1.0e38
EPS = float(np.float32(1e-5))
TWO_PI = 2.0 * math.pi
C1 = float(np.float32(6.28125))
C2 = float(np.float32(TWO_PI - np.float64(np.float32(6.28125))))
C3 = float(np.float32(TWO_PI - np.float64(np.float32(6.28125)) - np.float64(np.float32(C2))))
INV2PI = float(np.float32(1.0 / TWO_PI))
PI_F = float(np.float32(math.pi))
PI_DN = float(np.nextafter(np.float32(math.pi), np.float32(0)))
A1 = float(np.float32(math.pi / 2))
A2 = float(np.float32(math.pi / 2 - np.float64(np.float32(math.pi / 2))))

FREQ_BITS = [
    0x3f800000, 0x3f3ff911, 0x3f0ff59a, 0x3ed7e89a, 0x3ea1e89b, 0x3e72d422, 0x3e361887, 0x3e088d77,
    0x3dcccccc, 0x3d99940d, 0x3d6655c1, 0x3d2cba16, 0x3d0186e2, 0x3cc2434e, 0x3c91ad3a, 0x3c5a7bf1,
    0x3c23d70a, 0x3bf5b9ad, 0x3bb8449a, 0x3b8a2e75, 0x3b4f3e33, 0x3b1b690e, 0x3ae91528, 0x3aaec98d,
    0x3a83126e, 0x3a44948a, 0x3a136a14, 0x39dd1722, 0x39a5cb60, 0x3978a80d, 0x393a7753, 0x390bd46d,
]


_STOP_AFTER_FFN = False
_DEBUG_DUMPS = False
_DEBUG_TILE0 = False
_DEBUG_ATT = False
_DEBUG_WO = False
_DEBUG_CANARY = False
_DEBUG_NOATTN = False
_DEBUG_PSUMONLY = False
_DEBUG_SPLITADD = False
_DEBUG_STTADD = False
_PARANOID = False
_NOPS = 0


class Sem:
    def __init__(self, handle, name):
        self.h = handle
        self.name = name
        self.n = 0


class Eng:
    def __init__(self, name, sem):
        self.name = name
        self.sem = sem
        self.ops = []
        self.seen = {}
        self.trace = []
        self.drained = 0

    def wait(self, sem, val):
        if val <= 0 or sem is self.sem:
            return
        if self.seen.get(sem.name, 0) >= val:
            return
        self.seen[sem.name] = val
        self.trace.append(("w", sem.name, val))
        h = sem.h
        self.ops.append(lambda e: e.wait_ge(h, val))

    def drain(self):
        if self.sem.n <= 0 or self.drained >= self.sem.n:
            return
        self.drained = self.sem.n
        h, val = self.sem.h, self.sem.n
        self.trace.append(("w", self.sem.name, val))
        self.ops.append(lambda e: e.wait_ge(h, val))

    def op(self, fn):
        self.ops.append(fn)

    def op_inc(self, fn, k=1):
        self.sem.n += k
        h, val = self.sem.h, self.sem.n
        self.trace.append(("i", self.sem.name, k))
        self.ops.append(lambda e: fn(e).then_inc(h, k))
        self.seen[self.sem.name] = self.sem.n
        return (self.sem, val)


class Buf:
    __slots__ = ("name", "w", "rs")

    def __init__(self, name=""):
        self.name = name
        self.w = None
        self.rs = {}


def dep_read(eng, *bufs):
    for b in bufs:
        if b.w is not None:
            if _PARANOID and b.w[0] is eng.sem and eng.drained < b.w[1]:
                eng.drain()
            eng.wait(*b.w)


def dep_write(eng, *bufs):
    for b in bufs:
        if b.w is not None:
            if _PARANOID and b.w[0] is eng.sem and eng.drained < b.w[1]:
                eng.drain()
            eng.wait(*b.w)
        for sem, val in b.rs.values():
            if _PARANOID and sem is eng.sem and eng.drained < val:
                eng.drain()
            eng.wait(sem, val)


def note_write(mark, *bufs):
    for b in bufs:
        b.w = mark
        b.rs = {}


def note_read(mark, *bufs):
    sem, val = mark
    for b in bufs:
        old = b.rs.get(sem.name)
        if old is None or old[1] < val:
            b.rs[sem.name] = (sem, val)


def build_nc():
    nc = bass.Bass("TRN2", target_bir_lowering=False, detect_race_conditions=False)

    dt = lambda name, shape, dtype=F32, kind="ExternalInput": \
        nc.dram_tensor(name, shape, dtype, kind=kind).ap()

    x_h = dt("x", [S, E])
    wdkv_h = dt("wdkv", [E, DKV])
    wdq_h = dt("wdq", [E, DKV])
    wkr_h = dt("wkr", [E, DR])
    wqc_h = dt("wqc", [H, DKV, DC])
    wqr_h = dt("wqr", [H, DKV, DR])
    wuk_h = dt("wuk", [H, DKV, DC])
    wuv_h = dt("wuv", [H, DKV, DH])
    wor_h = dt("wor", [4 * 128, 16 * 512])
    gwt_h = dt("gwt", [NI * 128, E])
    uwt_h = dt("uwt", [NI * 128, E])
    dwt_h = dt("dwt", [11 * 128, 4 * E])
    gs_h = dt("gs", [128, NI])
    us_h = dt("us", [128, NI])
    dsb_h = dt("dsb", [128, E])
    ident_h = dt("ident", [128, 128], BF16)
    bmask_h = dt("bmask", [128, 4 * 512], BF16)
    ang_h = dt("ang", [64, S])
    out_h = dt("out", [S, E], F32, kind="ExternalOutput")
    part_h = dt("part", [S, E], F32, kind="Internal")
    if _DEBUG_CANARY:
        canary_h = dt("canary", [128, 8], F32, kind="ExternalOutput")
    dbg_h = {}
    if _DEBUG_DUMPS:
        if _DEBUG_ATT:
            dbg_h["d_kb"] = dt("d_kb", [128, S], BF16, kind="ExternalOutput")
            dbg_h["d_vb"] = dt("d_vb", [128, S], BF16, kind="ExternalOutput")
            dbg_h["d_qb"] = dt("d_qb", [128, S], BF16, kind="ExternalOutput")
            dbg_h["d_pb"] = dt("d_pb", [128, S], BF16, kind="ExternalOutput")
            dbg_h["d_ssc"] = dt("d_ssc", [128, S], F32, kind="ExternalOutput")
            dbg_h["d_ptb"] = dt("d_ptb", [128, 16 * 512], BF16, kind="ExternalOutput")
            dbg_h["d_ctx"] = dt("d_ctx", [128, S], BF16, kind="ExternalOutput")
            dbg_h["d_smx"] = dt("d_smx", [128, 8], F32, kind="ExternalOutput")
        if _DEBUG_WO:
            dbg_h["d_ctxall"] = dt("d_ctxall", [128, H * S], BF16, kind="ExternalOutput")
            dbg_h["d_rb"] = dt("d_rb", [128, 512], F32, kind="ExternalOutput")
            dbg_h["d_ob"] = dt("d_ob", [128, 512], F32, kind="ExternalOutput")
        if _DEBUG_TILE0:
            dbg_h["d_xt0"] = dt("d_xt0", [128, E], F32, kind="ExternalOutput")
            dbg_h["d_xn0"] = dt("d_xn0", [128, E], BF16, kind="ExternalOutput")
            dbg_h["d_st0"] = dt("d_st0", [128, 64], F32, kind="ExternalOutput")
        for nm, shape, dtp in [("d_sin", [64, S], BF16), ("d_cos", [64, S], BF16),
                               ("d_xnT", [128, NE * HALF], BF16), ("d_ckvT", [128, S], BF16),
                               ("d_xdqT", [128, S], BF16), ("d_krT", [64, S], BF16)]:
            dbg_h[nm] = dt(nm, shape, dtp, kind="ExternalOutput")

    es = ExitStack()
    SBt = lambda name, shape, dtp=F32: es.enter_context(nc.sbuf_tensor(name, shape, dtp))

    ident = SBt("identsb", [128, 128], BF16)
    bmask = SBt("bmask_sb", [128, 4 * 512], BF16)
    sinT = SBt("sinT", [64, S], BF16)
    cosT = SBt("cosT", [64, S], BF16)
    scr_a = SBt("scr_a", [64, 512])
    scr_b = SBt("scr_b", [64, 512])
    zcol = SBt("zcol", [128, 128], BF16)
    w_qc = SBt("w_qc", [128, H * DC], BF16)
    w_qr = SBt("w_qr", [128, H * DR], BF16)
    w_uk = SBt("w_uk", [128, H * DC], BF16)
    w_uv = SBt("w_uv", [128, H * DH], BF16)
    gs_sb = SBt("gs_sb", [128, NI])
    us_sb = SBt("us_sb", [128, NI])
    c_kvT = SBt("c_kvT", [128, S], BF16)
    x_dqT = SBt("x_dqT", [128, S], BF16)
    krT = SBt("krT", [64, S], BF16)
    stats = SBt("stats", [128, 64])

    ph = ExitStack()
    SBp = lambda name, shape, dtp=F32: ph.enter_context(nc.sbuf_tensor(name, shape, dtp))
    angb = SBp("angb", [64, S])
    scr_c = SBp("scr_c", [64, 512])
    scr_i = SBp("scr_i", [64, 512], I32)
    w_dkv = SBp("w_dkv", [128, NE * DKV], BF16)
    w_dq = SBp("w_dq", [128, NE * DKV], BF16)
    w_kr = SBp("w_kr", [128, NE * DR], BF16)
    dsb = SBp("dsb_sb", [128, E])
    xnT = SBp("xnT", [128, NE * HALF], BF16)
    acc = SBp("acc", [128, 8 * E], BF16)
    xt0 = SBp("xt0", [128, E])
    xt1 = SBp("xt1", [128, E])
    xnat0 = SBp("xnat0", [128, E], BF16)
    xnat1 = SBp("xnat1", [128, E], BF16)
    gwt = [SBp(f"gwt{i}", [128, E], BF16) for i in range(2)]
    uwt = [SBp(f"uwt{i}", [128, E], BF16) for i in range(2)]
    hdnT = [SBp(f"hdnT{i}", [128, HALF], BF16) for i in range(4)]
    dwt = [SBp("dwt0", [128, 4 * E], BF16)]
    silu = SBp("silu", [128, 1024])
    rtmp = SBp("rtmp", [128, 1024])

    p_tr = ph.enter_context(nc.psum_tensor("p_tr", [128, 1024], F32))
    p_g = ph.enter_context(nc.psum_tensor("p_g", [128, 1024], F32))
    p_u = ph.enter_context(nc.psum_tensor("p_u", [128, 1024], F32))
    p_d0 = ph.enter_context(nc.psum_tensor("p_d0", [128, 512], F32))
    p_d1 = ph.enter_context(nc.psum_tensor("p_d1", [128, 512], F32))

    sem_es = ExitStack()
    mk_sem = lambda name: Sem(sem_es.enter_context(nc.semaphore(name)), name)
    s_pe = mk_sem("s_pe")
    s_dv = mk_sem("s_dv")
    s_sc = mk_sem("s_sc")
    s_cs = mk_sem("s_cs")
    s_cg = mk_sem("s_cg")
    s_x = [mk_sem("s_x0"), mk_sem("s_x1")]
    s_gw = [mk_sem("s_gw0"), mk_sem("s_gw1")]
    s_uw = [mk_sem("s_uw0"), mk_sem("s_uw1")]
    s_dw = [mk_sem("s_dw0"), mk_sem("s_dw1")]
    s_wo = [mk_sem("s_wo0"), mk_sem("s_wo1")]
    s_rb = [mk_sem("s_rb0"), mk_sem("s_rb1")]
    s_ob = [mk_sem("s_ob0"), mk_sem("s_ob1")]
    s_pw = mk_sem("s_pw")
    s_st = mk_sem("s_st")
    all_dma_sems = [s_cs, s_cg, s_x[0], s_x[1], s_gw[0], s_gw[1], s_uw[0], s_uw[1],
                    s_dw[0], s_dw[1], s_wo[0], s_wo[1], s_rb[0], s_rb[1], s_ob[0], s_ob[1], s_pw, s_st]

    PE = Eng("pe", s_pe)
    DV = Eng("dv", s_dv)
    SC = Eng("sc", s_sc)
    SY = Eng("sy", s_cs)
    GP = Eng("gp", s_cg)

    def dma(eng, sem, src_ap, dst_ap, rd_bufs=(), wr_bufs=()):
        dep_read(eng, *rd_bufs)
        dep_write(eng, *wr_bufs)
        sem.n += 16
        h, val = sem.h, sem.n
        eng.trace.append(("i", sem.name, 16))
        eng.ops.append(lambda e: e.dma_start(out=dst_ap, in_=src_ap).then_inc(h, 16))
        mark = (sem, val)
        note_write(mark, *wr_bufs)
        note_read(mark, *rd_bufs)
        return mark

    def run(eng, fns, rd=(), wr=(), k=1):
        dep_read(eng, *rd)
        dep_write(eng, *wr)
        for fn in fns[:-1]:
            eng.op(fn)
        mark = eng.op_inc(fns[-1], k)
        note_write(mark, *wr)
        note_read(mark, *rd)
        return mark

    B = lambda n: Buf(n)
    b_const_sy = B("const_sy")
    b_const_gp = B("const_gp")
    b_sin, b_cos = B("sin"), B("cos")
    b_scr = B("scr")
    b_ckvT, b_xdqT, b_krT = B("ckvT"), B("xdqT"), B("krT")
    b_stats = B("stats")
    b_xnT = [B("xnT0"), B("xnT1")]
    b_acc = [B(f"acc{t}") for t in range(8)]
    b_xt = [B("xt0"), B("xt1")]
    b_xnat = [B("xnat0"), B("xnat1")]
    b_gwt = [B("gwt0"), B("gwt1")]
    b_uwt = [B("uwt0"), B("uwt1")]
    b_hdnT = [B(f"hdnT{i}") for i in range(4)]
    b_dwt = [B("dwt0")]
    b_silu = B("silu")
    b_rtmp = B("rtmp")
    b_ptr = [B(f"p_tr{j}") for j in range(2)]
    b_pg, b_pu = B("p_g"), B("p_u")
    b_pd = [B("p_d0"), B("p_d1")]

    dma(SY, s_cs, ident_h, ident[:], wr_bufs=[b_const_sy])
    dma(SY, s_cs, bmask_h, bmask[:], wr_bufs=[b_const_sy])
    dma(SY, s_cs, ang_h, angb[:], wr_bufs=[b_const_sy])
    dma(SY, s_cs, gs_h, gs_sb[:], wr_bufs=[b_const_sy])
    dma(SY, s_cs, us_h, us_sb[:], wr_bufs=[b_const_sy])
    dma(SY, s_cs, dsb_h, dsb[:], wr_bufs=[b_const_sy])
    dma(GP, s_gw[0], gwt_h[0:128, :], gwt[0][:], wr_bufs=[b_gwt[0]])
    dma(GP, s_uw[0], uwt_h[0:128, :], uwt[0][:], wr_bufs=[b_uwt[0]])
    dma(GP, s_dw[0], dwt_h[0:128, :], dwt[0][:], wr_bufs=[b_dwt[0]])
    dma(GP, s_gw[1], gwt_h[128:256, :], gwt[1][:], wr_bufs=[b_gwt[1]])
    dma(GP, s_uw[1], uwt_h[128:256, :], uwt[1][:], wr_bufs=[b_uwt[1]])
    for ec in range(NE):
        dma(GP, s_cg, wdkv_h[ec * 128:(ec + 1) * 128, :], w_dkv[:, ec * DKV:(ec + 1) * DKV], wr_bufs=[b_const_gp])
        dma(GP, s_cg, wdq_h[ec * 128:(ec + 1) * 128, :], w_dq[:, ec * DKV:(ec + 1) * DKV], wr_bufs=[b_const_gp])
        dma(GP, s_cg, wkr_h[ec * 128:(ec + 1) * 128, :], w_kr[:, ec * DR:(ec + 1) * DR], wr_bufs=[b_const_gp])
    b_const_att = B("const_att")

    def emit_att_const_dmas():
        for h in range(H):
            dma(GP, s_cg, wqc_h[h], w_qc[:, h * DC:(h + 1) * DC], wr_bufs=[b_const_att])
            dma(GP, s_cg, wqr_h[h], w_qr[:, h * DR:(h + 1) * DR], wr_bufs=[b_const_att])
            dma(GP, s_cg, wuk_h[h], w_uk[:, h * DC:(h + 1) * DC], wr_bufs=[b_const_att])
            dma(GP, s_cg, wuv_h[h], w_uv[:, h * DH:(h + 1) * DH], wr_bufs=[b_const_att])

    run(DV, [lambda e: e.memset(stats[:, 42:43], EPS)], wr=[b_stats])
    b_zcol = B("zcol")
    run(DV, [lambda e: e.memset(zcol[:], 0.0)], wr=[b_zcol])
    for _ in range(_NOPS):
        run(DV, [lambda e: e.memset(stats[:, 43:44], 0.0)], wr=[b_stats])

    def emit_rope_tables():
        for blk in range(S // 512):
            sl = slice(blk * 512, blk * 512 + 512)
            for is_cos in (False, True):
                base = 0.75 if is_cos else 0.5
                tgt, b_tgt = (cosT, b_cos) if is_cos else (sinT, b_sin)
                fns = [
                    lambda e, sl=sl, base=base: e.tensor_scalar(scr_a[:, 0:512], angb[:, sl], INV2PI, base, ALU.mult, ALU.add),
                    lambda e: e.tensor_copy(scr_i[:, 0:512], scr_a[:, 0:512]),
                    lambda e: e.tensor_copy(scr_b[:, 0:512], scr_i[:, 0:512]),
                    lambda e, sl=sl: e.scalar_tensor_tensor(out=scr_a[:, 0:512], in0=scr_b[:, 0:512], scalar=-C1, in1=angb[:, sl], op0=ALU.mult, op1=ALU.add),
                    lambda e: e.scalar_tensor_tensor(out=scr_c[:, 0:512], in0=scr_b[:, 0:512], scalar=-C2, in1=scr_a[:, 0:512], op0=ALU.mult, op1=ALU.add),
                    lambda e: e.scalar_tensor_tensor(out=scr_a[:, 0:512], in0=scr_b[:, 0:512], scalar=-C3, in1=scr_c[:, 0:512], op0=ALU.mult, op1=ALU.add),
                ]
                if is_cos:
                    fns.append(lambda e: e.tensor_scalar(scr_a[:, 0:512], scr_a[:, 0:512], A1, A2, ALU.add, ALU.add))
                fns += [
                    lambda e: e.tensor_scalar(scr_b[:, 0:512], scr_a[:, 0:512], -PI_F, None, ALU.is_lt),
                    lambda e: e.tensor_scalar(scr_c[:, 0:512], scr_a[:, 0:512], PI_F, None, ALU.is_gt),
                    lambda e: e.tensor_sub(scr_b[:, 0:512], scr_b[:, 0:512], scr_c[:, 0:512]),
                    lambda e: e.scalar_tensor_tensor(out=scr_a[:, 0:512], in0=scr_b[:, 0:512], scalar=TWO_PI, in1=scr_a[:, 0:512], op0=ALU.mult, op1=ALU.add),
                    lambda e: e.tensor_scalar(scr_a[:, 0:512], scr_a[:, 0:512], PI_DN, -PI_DN, ALU.min, ALU.max),
                ]
                run(DV, fns, rd=[b_const_sy], wr=[b_scr])
                run(SC, [
                    lambda e, tgt=tgt, sl=sl: e.activation(tgt[:, sl], scr_a[:, 0:512], ACT.Sin),
                ], rd=[b_scr], wr=[b_tgt])

    def dump(nm, sb_ap, rd):
        if _DEBUG_DUMPS:
            dma(SY, s_st, sb_ap, dbg_h[nm].tensor.ap() if hasattr(dbg_h[nm], "tensor") else dbg_h[nm], rd_bufs=[rd])

    def apply_rope(src, src_buf, src_col0, dst, dst_buf, t0, width, dst_row0=0, eng=None):
        eng = eng or DV
        for b0 in range(0, width, 512):
            w = min(512, width - b0)
            cs = slice(t0 + b0, t0 + b0 + w)
            ss = slice(src_col0 + b0, src_col0 + b0 + w)
            run(eng, [
                lambda e, ss=ss, w=w, cs=cs: e.tensor_mul(scr_a[0:32, 0:w], src[0:32, ss], cosT[0:32, cs]),
                lambda e, ss=ss, w=w, cs=cs: e.tensor_mul(scr_b[0:32, 0:w], src[32:64, ss], sinT[32:64, cs]),
                lambda e, w=w, cs=cs: e.tensor_sub(dst[dst_row0:dst_row0 + 32, cs], scr_a[0:32, 0:w], scr_b[0:32, 0:w]),
                lambda e, ss=ss, w=w, cs=cs: e.tensor_mul(scr_a[0:32, 0:w], src[0:32, ss], sinT[0:32, cs]),
                lambda e, ss=ss, w=w, cs=cs: e.tensor_mul(scr_b[0:32, 0:w], src[32:64, ss], cosT[32:64, cs]),
                lambda e, w=w, cs=cs: e.tensor_add(dst[dst_row0 + 32:dst_row0 + 64, cs], scr_a[0:32, 0:w], scr_b[0:32, 0:w]),
            ], rd=[src_buf, b_sin, b_cos], wr=[dst_buf, b_scr])

    def emit_residual_tile(half, tc, eng):
        toff = half * HALF
        xt, b_x = (xt0, xt1)[tc % 2], b_xt[tc % 2]
        tmp, b_t = (rtmp, b_rtmp) if eng is GP else (silu, b_silu)
        dma(SY, s_x[tc % 2], x_h[toff + tc * 128: toff + (tc + 1) * 128, :], xt[:], wr_bufs=[b_x])
        for q4 in range(2):
            sl = slice(q4 * 1024, q4 * 1024 + 1024)
            run(eng, [
                lambda e, tc=tc, sl=sl, tmp=tmp: e.tensor_mul(
                    tmp[:], acc[:, tc * E + sl.start: tc * E + sl.stop], dsb[:, sl]),
            ], rd=[b_acc[tc], b_const_sy], wr=[b_t])
            run(eng, [
                lambda e, xt=xt, sl=sl, tmp=tmp: e.tensor_add(xt[:, sl], xt[:, sl], tmp[:]),
            ], rd=[b_t], wr=[b_x])
        dma(SC, s_pw, xt[:], part_h[toff + tc * 128: toff + (tc + 1) * 128, :], rd_bufs=[b_x])

    for half in range(2):
        toff = half * HALF
        for i in range(8):
            xt = (xt0, xt1)[i % 2]
            b_x = b_xt[i % 2]
            xn_t = (xnat0, xnat1)[i % 2]
            b_xn = b_xnat[i % 2]
            dma(SY, s_x[i % 2], x_h[toff + i * 128: toff + (i + 1) * 128, :], xt[:], wr_bufs=[b_x])
            ms_ap = bass.AP(stats, 1, [[64, 128], [3, 8]])
            cv_ap = bass.AP(stats, 2, [[64, 128], [3, 8]])
            run(DV, [
                (lambda e, xt=xt, j=j: e.bn_stats(stats[:, 6 * j:6 * j + 6], xt[:, 512 * j:512 * j + 512]))
                for j in range(4)
            ], rd=[b_x], wr=[b_stats])
            DV.drain()
            run(DV, [
                lambda e, ms_ap=ms_ap: e.reduce_sum(stats[:, 24:25], ms_ap, AX.X),
                lambda e, cv_ap=cv_ap: e.reduce_sum(stats[:, 25:26], cv_ap, AX.X),
                lambda e, ms_ap=ms_ap: e.tensor_mul(stats[:, 32:40], ms_ap, ms_ap),
            ], rd=[b_stats], wr=[b_stats])
            DV.drain()
            run(DV, [
                lambda e: e.tensor_scalar(stats[:, 27:28], stats[:, 24:25], -0.125, None, ALU.mult),
                lambda e: e.tensor_scalar(stats[:, 28:29], stats[:, 25:26], 1.0 / 2048.0, None, ALU.mult),
                lambda e: e.reduce_sum(stats[:, 26:27], stats[:, 32:40], AX.X),
            ], rd=[b_stats], wr=[b_stats])
            DV.drain()
            run(DV, [
                lambda e: e.tensor_scalar(stats[:, 30:31], stats[:, 27:28], stats[:, 27:28], None, ALU.mult),
                lambda e: e.scalar_tensor_tensor(out=stats[:, 29:30], in0=stats[:, 26:27], scalar=0.125, in1=stats[:, 28:29], op0=ALU.mult, op1=ALU.add),
            ], rd=[b_stats], wr=[b_stats])
            DV.drain()
            run(DV, [
                lambda e: e.tensor_sub(stats[:, 31:32], stats[:, 29:30], stats[:, 30:31]),
            ], rd=[b_stats], wr=[b_stats])
            run(SC, [
                lambda e: e.activation(stats[:, 41:42], stats[:, 31:32], ACT.Sqrt, bias=stats[:, 42:43]),
            ], rd=[b_stats], wr=[b_stats])
            run(DV, [
                lambda e: e.reciprocal(stats[:, 40:41], stats[:, 41:42]),
            ], rd=[b_stats], wr=[b_stats])
            DV.drain()
            run(DV, [
                lambda e, xt=xt, xn_t=xn_t: e.tensor_scalar(xn_t[:], xt[:], stats[:, 27:28], stats[:, 40:41], ALU.add, ALU.mult),
            ], rd=[b_x, b_stats], wr=[b_xn])
            for hp in range(2):
                for bank in range(2):
                    last = (bank == 1)
                    fns = [
                        (lambda e, xn_t=xn_t, ec=8 * hp + 4 * bank + q, bank=bank, q=q, last=last:
                         e.matmul(p_tr[:, bank * 512 + q * 128: bank * 512 + (q + 1) * 128],
                                  xn_t[:, ec * 128:(ec + 1) * 128], ident[:],
                                  start=(q == 0), stop=(q == 3 and not last)))
                        for q in range(4)
                    ]
                    if last:
                        fns.append(lambda e, bank=bank: e.matmul(
                            p_tr[:, bank * 512: bank * 512 + 128], zcol[:],
                            ident[:], start=False, stop=True))
                    run(PE, fns, rd=[b_xn, b_const_sy, b_zcol], wr=[b_ptr[bank]])
                run(SC, [
                    lambda e, i=i, hp=hp: e.copy(
                        bass.AP(xnT, 8 * hp * HALF + i * 128, [[NE * HALF, 128], [HALF, 8], [1, 128]]),
                        p_tr[:].rearrange("p (a b) -> p a b", b=128)),
                ], rd=[b_ptr[0], b_ptr[1]], wr=[b_xnT[i // 4]])
            if _DEBUG_TILE0 and half == 0 and i == 0:
                dump("d_xt0", xt[:], b_x)
                dump("d_xn0", xn_t[:], b_xn)
                dump("d_st0", stats[:], b_stats)

        if half == 0:
            emit_rope_tables()
        for tb in range(2):
            gl = slice(toff + tb * 512, toff + tb * 512 + 512)
            run(PE, [
                (lambda e, ec=ec, tb=tb: e.matmul(p_g[:, tb * 512:(tb + 1) * 512],
                                                  w_dkv[:, ec * 128:(ec + 1) * 128],
                                                  xnT[:, ec * HALF + tb * 512: ec * HALF + tb * 512 + 512],
                                                  start=(ec == 0), stop=(ec == 15)))
                for ec in range(16)
            ], rd=[b_const_gp, b_xnT[tb]], wr=[b_pg])
            run(SC, [lambda e, gl=gl, tb=tb: e.copy(c_kvT[:, gl], p_g[:, tb * 512:(tb + 1) * 512])],
                rd=[b_pg], wr=[b_ckvT])
            run(PE, [
                (lambda e, ec=ec, tb=tb: e.matmul(p_u[:, tb * 512:(tb + 1) * 512],
                                                  w_dq[:, ec * 128:(ec + 1) * 128],
                                                  xnT[:, ec * HALF + tb * 512: ec * HALF + tb * 512 + 512],
                                                  start=(ec == 0), stop=(ec == 15)))
                for ec in range(16)
            ], rd=[b_const_gp, b_xnT[tb]], wr=[b_pu])
            run(SC, [lambda e, gl=gl, tb=tb: e.copy(x_dqT[:, gl], p_u[:, tb * 512:(tb + 1) * 512])],
                rd=[b_pu], wr=[b_xdqT])
            run(PE, [
                (lambda e, ec=ec, tb=tb: e.matmul(p_g[0:64, tb * 512:(tb + 1) * 512],
                                                  w_kr[:, ec * 64:(ec + 1) * 64],
                                                  xnT[:, ec * HALF + tb * 512: ec * HALF + tb * 512 + 512],
                                                  start=(ec == 0), stop=(ec == 15)))
                for ec in range(16)
            ], rd=[b_const_gp, b_xnT[tb]], wr=[b_pg])
            run(SC, [lambda e, tb=tb: e.copy(xt0[0:64, tb * 512:(tb + 1) * 512],
                                             p_g[0:64, tb * 512:(tb + 1) * 512])],
                rd=[b_pg], wr=[b_xt[0]])
        apply_rope(xt0, b_xt[0], 0, krT, b_krT, toff, HALF)

        if _DEBUG_DUMPS and half == 0:
            dump("d_xnT", xnT[:], b_xnT[0])

        if half == 1:
            emit_residual_tile(0, 0, GP)
            emit_residual_tile(0, 1, GP)

        for ic in range(NI):
            g = ic // 4
            sub = ic % 4
            gwc, bgw = gwt[ic % 2], b_gwt[ic % 2]
            uwc, buw = uwt[ic % 2], b_uwt[ic % 2]
            if sub == 0 and g >= 1:
                dma(GP, s_dw[g % 2], dwt_h[g * 128:(g + 1) * 128, :], dwt[0][:],
                    wr_bufs=[b_dwt[0]])
            for tb in range(2):
                run(PE, [
                    (lambda e, ec=ec, tb=tb, gwc=gwc: e.matmul(
                        p_g[:, tb * 512:(tb + 1) * 512], gwc[:, ec * 128:(ec + 1) * 128],
                        xnT[:, ec * HALF + tb * 512: ec * HALF + tb * 512 + 512],
                        start=(ec == 0), stop=(ec == 15)))
                    for ec in range(16)
                ], rd=[bgw, b_xnT[tb]], wr=[b_pg])
            for tb in range(2):
                run(PE, [
                    (lambda e, ec=ec, tb=tb, uwc=uwc: e.matmul(
                        p_u[:, tb * 512:(tb + 1) * 512], uwc[:, ec * 128:(ec + 1) * 128],
                        xnT[:, ec * HALF + tb * 512: ec * HALF + tb * 512 + 512],
                        start=(ec == 0), stop=(ec == 15)))
                    for ec in range(16)
                ], rd=[buw, b_xnT[tb]], wr=[b_pu])
            run(SC, [
                lambda e, ic=ic: e.activation(silu[:], p_g[:], ACT.Silu, scale=gs_sb[:, ic:ic + 1]),
            ], rd=[b_pg, b_const_sy], wr=[b_silu])
            run(DV, [
                lambda e, ic=ic, sub=sub: e.scalar_tensor_tensor(
                    out=hdnT[sub][:], in0=p_u[:],
                    scalar=us_sb[:, ic:ic + 1], in1=silu[:], op0=ALU.mult, op1=ALU.mult),
            ], rd=[b_pu, b_silu, b_const_sy], wr=[b_hdnT[sub]])

            if sub == 3:
                dwc, bdw = dwt[0], b_dwt[0]
                for tc in range(8):
                    if half == 1 and g == 0 and tc >= 5:
                        emit_residual_tile(0, tc, GP)
                    for hb in range(4):
                        k2 = (tc * 4 + hb) % 2
                        pd, bpd = (p_d0, p_d1)[k2], b_pd[k2]
                        run(PE, [
                            (lambda e, icq=icq, tc=tc, hb=hb, pd=pd, dwc=dwc:
                             e.matmul(pd[:], hdnT[icq][:, tc * 128:(tc + 1) * 128],
                                      dwc[:, icq * E + hb * 512: icq * E + hb * 512 + 512],
                                      start=(icq == 0), stop=(icq == 3)))
                            for icq in range(4)
                        ], rd=[b_hdnT[0], b_hdnT[1], b_hdnT[2], b_hdnT[3], bdw], wr=[bpd])
                        asl = slice(tc * E + hb * 512, tc * E + hb * 512 + 512)
                        if g == 0:
                            run(DV, [lambda e, asl=asl, pd=pd: e.tensor_copy(acc[:, asl], pd[:])],
                                rd=[bpd], wr=[b_acc[tc]])
                        else:
                            run(DV, [lambda e, asl=asl, pd=pd: e.scalar_tensor_tensor(
                                out=acc[:, asl], in0=pd[:], scalar=1.0, in1=acc[:, asl],
                                op0=ALU.mult, op1=ALU.add)], rd=[bpd], wr=[b_acc[tc]])

            icn = ic + 2
            if icn < NI:
                dma(GP, s_gw[icn % 2], gwt_h[icn * 128:(icn + 1) * 128, :],
                    gwt[icn % 2][:], wr_bufs=[b_gwt[icn % 2]])
                dma(GP, s_uw[icn % 2], uwt_h[icn * 128:(icn + 1) * 128, :],
                    uwt[icn % 2][:], wr_bufs=[b_uwt[icn % 2]])
            elif half == 0:
                icn -= NI
                dma(GP, s_gw[icn % 2], gwt_h[icn * 128:(icn + 1) * 128, :],
                    gwt[icn % 2][:], wr_bufs=[b_gwt[icn % 2]])
                dma(GP, s_uw[icn % 2], uwt_h[icn * 128:(icn + 1) * 128, :],
                    uwt[icn % 2][:], wr_bufs=[b_uwt[icn % 2]])
            if half == 1 and 0 <= ic <= 2:
                emit_residual_tile(0, ic + 2, GP)

        if half == 0:
            dma(GP, s_dw[0], dwt_h[0:128, :], dwt[0][:], wr_bufs=[b_dwt[0]])
            emit_att_const_dmas()

    for tc in range(8):
        emit_residual_tile(1, tc, (DV, GP)[tc % 2])

    if _DEBUG_DUMPS:
        dump("d_sin", sinT[:], b_sin)
        dump("d_cos", cosT[:], b_cos)
        dump("d_ckvT", c_kvT[:], b_ckvT)
        dump("d_xdqT", x_dqT[:], b_xdqT)
        dump("d_krT", krT[:], b_krT)

    ph.close()
    marks = [(s_pe, s_pe.n), (s_dv, s_dv.n), (s_sc, s_sc.n)] + \
            [(s, s.n) for s in all_dma_sems]
    for eng in (PE, DV, SC, SY, GP):
        for m in marks:
            eng.wait(*m)

    if _STOP_AFTER_FFN:
        for s_fin in (s_x[0], s_x[1], s_st):
            if s_fin.n > 0:
                SY.trace.append(("w", s_fin.name, s_fin.n))
                SY.ops.append(lambda e, h=s_fin.h, v=s_fin.n: e.wait_ge(h, v))
        build_nc.engines = dict(pe=PE, dv=DV, sc=SC, sy=SY, gp=GP)
        with es, sem_es, nc.Block() as block:
            @block.tensor
            def _(t):
                for f in PE.ops:
                    f(t)

            @block.vector
            def _(v):
                for f in DV.ops:
                    f(v)

            @block.scalar
            def _(s):
                for f in SC.ops:
                    f(s)

            @block.sync
            def _(s):
                for f in SY.ops:
                    f(s)

            @block.gpsimd
            def _(g):
                for f in GP.ops:
                    f(g)
        return nc

    at = ExitStack()
    SBa = lambda name, shape, dtp=F32: at.enter_context(nc.sbuf_tensor(name, shape, dtp))
    ctx_all = SBa("ctx_all", [128, H * S], BF16)
    kT = [SBa("kT0", [128, S], BF16), SBa("kT1", [128, S], BF16)]
    vb = [SBa("vb0", [128, S], BF16), SBa("vb1", [128, S], BF16)]
    qu = [SBa("qu0", [128, S], BF16), SBa("qu1", [128, S], BF16)]
    qrp = SBa("qrp", [64, S])
    ptb = [SBa("ptb0", [128, 16 * 512], BF16), SBa("ptb1", [128, 16 * 512], BF16)]
    Rb = [SBa("Rb0", [128, 512]), SBa("Rb1", [128, 512])]
    ones_sb = SBa("ones_sb", [128, 128], BF16)
    wob = SBa("wob", [128, 16 * 512], BF16)
    obuf = [SBa(f"ob{i}", [128, 512]) for i in range(4)]
    rbuf = [SBa(f"rb{i}", [128, 512]) for i in range(4)]

    p_s = at.enter_context(nc.psum_tensor("p_s", [128, 2048], F32))
    p_t1 = at.enter_context(nc.psum_tensor("p_t1", [128, 512], F32))
    p_t2 = at.enter_context(nc.psum_tensor("p_t2", [128, 512], F32))
    p_c1 = at.enter_context(nc.psum_tensor("p_c1", [128, 512], F32))
    p_c2 = at.enter_context(nc.psum_tensor("p_c2", [128, 512], F32))

    b_ctx = B("ctx")
    b_kT = [B("kT0"), B("kT1")]
    b_vb = [B("vb0"), B("vb1")]
    b_qu = [B("qu0"), B("qu1")]
    b_qrp = B("qrp")
    b_ptb = [B("ptb0"), B("ptb1")]
    b_R = [B("Rb0"), B("Rb1")]
    b_ones = B("ones_sb")
    b_wob = [B("wob0"), B("wob1")]
    b_ob = [B(f"ob{i}") for i in range(4)]
    b_rb = [B(f"rb{i}") for i in range(4)]
    b_ps = [B(f"p_s{j}") for j in range(4)]
    b_pt1, b_pt2, b_pc1, b_pc2 = B("p_t1"), B("p_t2"), B("p_c1"), B("p_c2")

    run(DV, [lambda e: e.memset(ones_sb[:], 1.0)], wr=[b_ones])
    for j in range(2):
        run(SC, [lambda e, j=j: e.copy(kT[j][64:128, :], krT[:])], rd=[b_krT], wr=[b_kT[j]])

    def emit_proj(h):
        kb, b_kb = kT[h % 2], b_kT[h % 2]
        vv, b_vv = vb[h % 2], b_vb[h % 2]
        qb, b_qb = qu[h % 2], b_qu[h % 2]
        for blk in range(4):
            pt, bpt = (p_t1, p_t2)[blk % 2], (b_pt1, b_pt2)[blk % 2]
            run(PE, [
                lambda e, h=h, blk=blk, pt=pt: e.matmul(
                    pt[0:64, :], w_uk[:, h * 64:(h + 1) * 64],
                    c_kvT[:, blk * 512:(blk + 1) * 512], start=True, stop=True),
            ], rd=[b_const_att, b_ckvT], wr=[bpt])
            run(SC, [
                lambda e, blk=blk, kb=kb, pt=pt: e.copy(kb[0:64, blk * 512:(blk + 1) * 512], pt[0:64, :]),
            ], rd=[bpt], wr=[b_kb])
        for blk4 in range(4):
            pc, bpc = (p_c1, p_c2)[blk4 % 2], (b_pc1, b_pc2)[blk4 % 2]
            run(PE, [
                (lambda e, h=h, j=j, blk4=blk4, pc=pc: e.matmul(
                    pc[:, j * 128:(j + 1) * 128],
                    c_kvT[:, (blk4 * 4 + j) * 128:(blk4 * 4 + j + 1) * 128],
                    w_uv[:, h * 128:(h + 1) * 128], start=True, stop=False))
                for j in range(4)
            ] + [
                lambda e, pc=pc: e.matmul(pc[:, 0:128], zcol[:], ident[:],
                                          start=False, stop=True),
            ], rd=[b_const_att, b_ckvT, b_const_sy, b_zcol], wr=[bpc])
            run(SC, [
                lambda e, blk4=blk4, vv=vv, pc=pc: e.copy(vv[:, blk4 * 512:(blk4 + 1) * 512], pc[:, 0:512]),
            ], rd=[bpc], wr=[b_vv])
        for blk in range(4):
            pt, bpt = (p_t1, p_t2)[blk % 2], (b_pt1, b_pt2)[blk % 2]
            run(PE, [
                lambda e, h=h, blk=blk, pt=pt: e.matmul(
                    pt[0:64, :], w_qc[:, h * 64:(h + 1) * 64],
                    x_dqT[:, blk * 512:(blk + 1) * 512], start=True, stop=True),
            ], rd=[b_const_att, b_xdqT], wr=[bpt])
            run(SC, [
                lambda e, blk=blk, qb=qb, pt=pt: e.copy(qb[0:64, blk * 512:(blk + 1) * 512], pt[0:64, :]),
            ], rd=[bpt], wr=[b_qb])
        for blk in range(4):
            pt, bpt = (p_t1, p_t2)[blk % 2], (b_pt1, b_pt2)[blk % 2]
            run(PE, [
                lambda e, h=h, blk=blk, pt=pt: e.matmul(
                    pt[0:64, :], w_qr[:, h * 64:(h + 1) * 64],
                    x_dqT[:, blk * 512:(blk + 1) * 512], start=True, stop=True),
            ], rd=[b_const_att, b_xdqT], wr=[bpt])
            run(SC, [
                lambda e, blk=blk, pt=pt: e.copy(qrp[0:64, blk * 512:(blk + 1) * 512], pt[0:64, :]),
            ], rd=[bpt], wr=[b_qrp])
        apply_rope(qrp, b_qrp, 0, qb, b_qb, 0, S, dst_row0=64, eng=GP)

    def emit_scores(qblk, kb, b_kb, qb, b_qb):
        nkc = 4 * qblk + 4
        pt_par, b_pt_par = ptb[qblk % 2], b_ptb[qblk % 2]
        for kc in range(nkc):
            o = kc - 4 * qblk
            c0 = o * 128 if o > 0 else 0
            psl = slice((kc % 4) * 512 + c0, (kc % 4) * 512 + 512)
            qsl = slice(qblk * 512 + c0, qblk * 512 + 512)
            tsl = slice(kc * 512 + c0, kc * 512 + 512)
            run(PE, [
                lambda e, psl=psl, qsl=qsl, kc=kc, kb=kb, qb=qb: e.matmul(
                    p_s[:, psl], kb[:, kc * 128:(kc + 1) * 128], qb[:, qsl],
                    start=True, stop=True),
            ], rd=[b_qb, b_kb], wr=[b_ps[kc % 4]])
            run(SC, [
                lambda e, tsl=tsl, psl=psl, pt_par=pt_par: e.activation(
                    pt_par[:, tsl], p_s[:, psl], ACT.Exp, scale=INV_SQRT_DH),
            ], rd=[b_ps[kc % 4]], wr=[b_pt_par])
        fns = []
        for o in range(4):
            kc = 4 * qblk + o
            if o > 0:
                fns.append(lambda e, kc=kc, o=o, pt_par=pt_par: e.memset(
                    pt_par[:, kc * 512: kc * 512 + o * 128], 0.0))
            fns.append(lambda e, kc=kc, o=o, pt_par=pt_par: e.tensor_mul(
                pt_par[:, kc * 512 + o * 128: kc * 512 + (o + 1) * 128],
                pt_par[:, kc * 512 + o * 128: kc * 512 + (o + 1) * 128],
                bmask[:, o * 512 + o * 128: o * 512 + (o + 1) * 128]))
        run(DV, fns, rd=[b_const_sy], wr=[b_pt_par])

    def emit_sums(qblk):
        nkc = 4 * qblk + 4
        pt_par, b_pt_par = ptb[qblk % 2], b_ptb[qblk % 2]
        psum_t, b_psum = (p_t1, p_t2)[qblk % 2], (b_pt1, b_pt2)[qblk % 2]
        run(PE, [
            (lambda e, kc=kc, psum_t=psum_t, pt_par=pt_par, nkc=nkc: e.matmul(
                psum_t[:], ones_sb[:], pt_par[:, kc * 512:(kc + 1) * 512],
                start=(kc == 0), stop=(kc == nkc - 1)))
            for kc in range(nkc)
        ], rd=[b_pt_par, b_ones], wr=[b_psum])
        run(DV, [
            lambda e, qblk=qblk, psum_t=psum_t: e.reciprocal(Rb[qblk % 2][:], psum_t[:]),
        ], rd=[b_psum], wr=[b_R[qblk % 2]])

    def emit_ctx(h, qblk, vv, b_vv):
        nkc = 4 * qblk + 4
        pt_par, b_pt_par = ptb[qblk % 2], b_ptb[qblk % 2]
        pc, bpc = (p_c1, p_c2)[qblk % 2], (b_pc1, b_pc2)[qblk % 2]
        run(PE, [
            (lambda e, kc=kc, vv=vv, pc=pc, nkc=nkc, pt_par=pt_par:
             e.matmul(pc[:], vv[:, kc * 128:(kc + 1) * 128],
                      pt_par[:, kc * 512:(kc + 1) * 512], start=(kc == 0), stop=(kc == nkc - 1)))
            for kc in range(nkc)
        ], rd=[b_vv, b_pt_par], wr=[bpc])
        run(DV, [
            lambda e, h=h, qblk=qblk, pc=pc: e.tensor_mul(
                ctx_all[:, h * S + qblk * 512: h * S + (qblk + 1) * 512],
                pc[:], Rb[qblk % 2][:]),
        ], rd=[bpc, b_R[qblk % 2]], wr=[b_ctx])

    emit_proj(0)
    for h in range(H):
        kb, b_kb = kT[h % 2], b_kT[h % 2]
        vv, b_vv = vb[h % 2], b_vb[h % 2]
        qb, b_qb = qu[h % 2], b_qu[h % 2]
        if h + 1 < H:
            emit_proj(h + 1)
        emit_scores(0, kb, b_kb, qb, b_qb)
        emit_scores(1, kb, b_kb, qb, b_qb)
        emit_sums(0)
        emit_ctx(h, 0, vv, b_vv)
        emit_scores(2, kb, b_kb, qb, b_qb)
        emit_sums(1)
        emit_ctx(h, 1, vv, b_vv)
        emit_scores(3, kb, b_kb, qb, b_qb)
        emit_sums(2)
        emit_ctx(h, 2, vv, b_vv)
        emit_sums(3)
        emit_ctx(h, 3, vv, b_vv)

    if _DEBUG_ATT:
        dump("d_ctx", ctx_all[:, 0:S], b_ctx)
    if _DEBUG_WO:
        dump("d_ctxall", ctx_all[:], b_ctx)

    b_po = [b_ps[0], b_ps[1], b_ps[2], b_ps[3], b_pt1, b_pt2, b_pc1, b_pc2]
    po_t = [p_s[:, 0:512], p_s[:, 512:1024], p_s[:, 1024:1536], p_s[:, 1536:2048],
            p_t1[:], p_t2[:], p_c1[:], p_c2[:]]
    for eb in range(4):
        for hf in range(2):
            dma(GP, s_wo[hf], wor_h[eb * 128:(eb + 1) * 128, hf * 4096:(hf + 1) * 4096],
                wob[:, hf * 4096:(hf + 1) * 4096], wr_bufs=[b_wob[hf]])
        for grp in range(2):
            for tc8 in range(4):
                tc = grp * 8 + tc8
                dma(SY, s_rb[tc8 % 2], part_h[tc * 128:(tc + 1) * 128, eb * 512:(eb + 1) * 512],
                    rbuf[tc8 % 4][:], wr_bufs=[b_rb[tc8 % 4]])
            for h in range(H):
                run(PE, [
                    (lambda e, h=h, tc8=tc8, grp=grp:
                     e.matmul(po_t[tc8],
                              ctx_all[:, h * S + (grp * 8 + tc8) * 128: h * S + (grp * 8 + tc8) * 128 + 128],
                              wob[:, h * 512:(h + 1) * 512], start=(h == 0), stop=(h == 15)))
                    for tc8 in range(8)
                ], rd=[b_ctx, b_wob[h // 8]], wr=[b_po[j] for j in range(8)])
            for tc8 in range(8):
                tc = grp * 8 + tc8
                rb, b_r = rbuf[tc8 % 4], b_rb[tc8 % 4]
                ob, b_o = obuf[tc8 % 4], b_ob[tc8 % 4]
                run(DV, [
                    lambda e, tc8=tc8, rb=rb, ob=ob: e.scalar_tensor_tensor(
                        out=ob[:], in0=po_t[tc8], scalar=1.0, in1=rb[:],
                        op0=ALU.mult, op1=ALU.add),
                ], rd=[b_r, b_po[tc8]], wr=[b_o])
                dma(SC, s_ob[tc8 % 2], ob[:], out_h[tc * 128:(tc + 1) * 128, eb * 512:(eb + 1) * 512], rd_bufs=[b_o])
                if tc8 < 4:
                    tcn = grp * 8 + tc8 + 4
                    dma(SY, s_rb[tc8 % 2], part_h[tcn * 128:(tcn + 1) * 128, eb * 512:(eb + 1) * 512],
                        rbuf[tc8 % 4][:], wr_bufs=[b_rb[tc8 % 4]])

    for s_fin in (s_x[0], s_x[1], s_ob[0], s_ob[1], s_st):
        if s_fin.n > 0:
            SY.trace.append(("w", s_fin.name, s_fin.n))
            SY.ops.append(lambda e, h=s_fin.h, v=s_fin.n: e.wait_ge(h, v))

    build_nc.engines = dict(pe=PE, dv=DV, sc=SC, sy=SY, gp=GP)

    with es, at, sem_es, nc.Block() as block:
        @block.tensor
        def _(t):
            for f in PE.ops:
                f(t)

        @block.vector
        def _(v):
            for f in DV.ops:
                f(v)

        @block.scalar
        def _(s):
            for f in SC.ops:
                f(s)

        @block.sync
        def _(s):
            for f in SY.ops:
                f(s)

        @block.gpsimd
        def _(g):
            for f in GP.ops:
                f(g)

    return nc


def make_feed(L0_attn_x, L0_attn_W_dkv, L0_attn_W_dq, L0_attn_W_kr, L0_attn_W_qc,
              L0_attn_W_qr, L0_attn_W_uk, L0_attn_W_uv, L0_attn_W_o,
              L0_ffn_gate_w, L0_ffn_up_w, L0_ffn_down_w, L0_ffn_gate_s,
              L0_ffn_up_s, L0_ffn_down_s):
    f32 = np.float32
    perm = np.concatenate([np.arange(0, 64, 2), np.arange(1, 64, 2)])
    freqs = np.array(FREQ_BITS, dtype=np.uint32).view(np.float32)
    ang = (freqs[:, None] * np.arange(S, dtype=np.float32)[None, :]).astype(np.float32)
    ang = np.concatenate([ang, ang], axis=0)
    ident = np.eye(128, dtype=np.float32).astype(ml_dtypes.bfloat16)
    bmask = np.zeros((128, 4, 4, 128), dtype=np.float32)
    triu = np.triu(np.ones((128, 128), dtype=np.float32))
    for o in range(4):
        for qc in range(4):
            if qc > o:
                bmask[:, o, qc, :] = 1.0
            elif qc == o:
                bmask[:, o, qc, :] = triu
    bmask = np.ascontiguousarray(bmask.reshape(128, 2048)).astype(ml_dtypes.bfloat16)

    gwt = np.asarray(L0_ffn_gate_w, dtype=f32).reshape(NI, 128, NE, 128) \
        .transpose(0, 3, 2, 1).reshape(NI * 128, E)
    uwt = np.asarray(L0_ffn_up_w, dtype=f32).reshape(NI, 128, NE, 128) \
        .transpose(0, 3, 2, 1).reshape(NI * 128, E)
    dwt = np.ascontiguousarray(np.asarray(L0_ffn_down_w, dtype=f32).T) \
        .reshape(11, 4, 128, E).transpose(0, 2, 1, 3).reshape(11 * 128, 4 * E)
    shared = {
        "wdkv": np.ascontiguousarray(L0_attn_W_dkv, dtype=f32),
        "wdq": np.ascontiguousarray(L0_attn_W_dq, dtype=f32),
        "wkr": np.ascontiguousarray(np.asarray(L0_attn_W_kr, dtype=f32)[:, perm]),
        "wqc": np.ascontiguousarray(L0_attn_W_qc, dtype=f32),
        "wqr": np.ascontiguousarray(np.asarray(L0_attn_W_qr, dtype=f32)[:, :, perm]),
        "wuk": np.ascontiguousarray(L0_attn_W_uk, dtype=f32),
        "wuv": np.ascontiguousarray(L0_attn_W_uv, dtype=f32),
        "wor": np.ascontiguousarray(np.asarray(L0_attn_W_o, dtype=f32)
                                    .reshape(16, 128, 4, 512).transpose(2, 1, 0, 3)
                                    .reshape(4 * 128, 16 * 512)),
        "gwt": np.ascontiguousarray(gwt),
        "uwt": np.ascontiguousarray(uwt),
        "dwt": np.ascontiguousarray(dwt),
        "gs": np.ascontiguousarray(np.asarray(L0_ffn_gate_s, dtype=f32).reshape(NI, 128).T),
        "us": np.ascontiguousarray(np.asarray(L0_ffn_up_s, dtype=f32).reshape(NI, 128).T),
        "dsb": np.ascontiguousarray(np.broadcast_to(
            np.asarray(L0_ffn_down_s, dtype=f32)[None, :], (128, E))),
        "ident": ident,
        "bmask": bmask,
        "ang": ang,
    }
    xs = np.ascontiguousarray(np.asarray(L0_attn_x, dtype=f32))
    return shared, xs


def kernel(**inputs):
    shared, xs = make_feed(**inputs)
    in_maps = [dict(shared, x=xs[b]) for b in range(8)]
    nc = build_nc()
    res = run_bass_kernel_spmd(nc, in_maps, core_ids=list(range(8)))
    return np.stack([np.asarray(r["out"], dtype=np.float32) for r in res.results], axis=0)
